# version 20
# baseline (speedup 1.0000x reference)
import math
import numpy as np
import concourse.bass as bass
import concourse.mybir as mybir
from concourse.bass_utils import run_bass_kernel_spmd
from contextlib import ExitStack

F32 = mybir.dt.float32
BF16 = mybir.dt.bfloat16
I32 = mybir.dt.int32
AF = mybir.ActivationFunctionType
ALU = mybir.AluOpType
AX = mybir.AxisListType

B, S, E = 4, 2048, 2048
HEADS, HD = 16, 128
P = 4
KTOP = 64
I_FFN = 8192
NT = 16
NOWN = 8
GH = 2
NG = HEADS // GH
SCALE = 1.0 / math.sqrt(HD)
NEG = -1.0e30
NEGT = -1.0e29
EPS = 1e-5
N_IT = I_FFN // 128
DN_BLK = 8
WMAX = 2176


def wj(j):
    return 256 * j + 384


def build_program():
    nc = bass.Bass("TRN2", target_bir_lowering=False, dynamic_dma_scratch_size=512)

    xfull = nc.dram_tensor("xfull", [S, E], F32, kind="ExternalInput")
    xown = nc.dram_tensor("xown", [NOWN * 128, E], F32, kind="ExternalInput")
    Wq_h = nc.dram_tensor("Wq", [E, E], F32, kind="ExternalInput")
    Wk_h = nc.dram_tensor("Wk", [E, E], F32, kind="ExternalInput")
    Wv_h = nc.dram_tensor("Wv", [E, E], F32, kind="ExternalInput")
    Wo_h = nc.dram_tensor("Wo", [E, E], F32, kind="ExternalInput")
    bqkT = nc.dram_tensor("bqkT", [128, 2 * NT], F32, kind="ExternalInput")
    bv_r = nc.dram_tensor("bv_r", [1, E], F32, kind="ExternalInput")
    bo_r = nc.dram_tensor("bo_r", [1, E], F32, kind="ExternalInput")
    pkT_h = nc.dram_tensor("pkT", [128, NT * P], F32, kind="ExternalInput")
    pv_h = nc.dram_tensor("pv_r", [P, E], F32, kind="ExternalInput")
    qoff_h = nc.dram_tensor("qoff", [128, 1], F32, kind="ExternalInput")
    gw_h = nc.dram_tensor("gw", [I_FFN, E], I32, kind="ExternalInput")
    uw_h = nc.dram_tensor("uw", [I_FFN, E], I32, kind="ExternalInput")
    dw_h = nc.dram_tensor("dwT", [I_FFN, E], I32, kind="ExternalInput")
    gsT_h = nc.dram_tensor("gsT", [128, N_IT], F32, kind="ExternalInput")
    usT_h = nc.dram_tensor("usT", [128, N_IT], F32, kind="ExternalInput")
    dsr_h = nc.dram_tensor("ds_r", [1, E], F32, kind="ExternalInput")
    out_h = nc.dram_tensor("out", [NOWN * 128, E], F32, kind="ExternalOutput")

    es_glob = ExitStack()

    def mkg(name, shape, dt):
        return es_glob.enter_context(nc.sbuf_tensor(name, shape, dt))

    acc = mkg("acc", [128, NOWN, E], F32)
    big = mkg("big", [128, NT, E], BF16)
    pool = mkg("pool", [128, E], F32)
    wcast = mkg("wcast", [128, E], BF16)
    wbf = wcast[:, 0:E]
    stats = mkg("stats", [128, 6, 24], F32)
    cols = mkg("cols", [128, 8], F32)
    biasq = mkg("biasq", [128, 2 * NT], F32)
    gscol = mkg("gscol", [128, N_IT], F32)
    uscol = mkg("uscol", [128, N_IT], F32)
    brow = mkg("brow", [1, 256], BF16)
    pkcol = mkg("pkcol", [128, NT, P], F32)
    qoff = mkg("qoff_sb", [128, 1], F32)

    ps_s = es_glob.enter_context(nc.psum_tensor("ps_s", [128, WMAX], F32))
    ps_a = es_glob.enter_context(nc.psum_tensor("ps_a", [128, 512], F32))
    ps_b = es_glob.enter_context(nc.psum_tensor("ps_b", [128, 512], F32))
    ps_c = es_glob.enter_context(nc.psum_tensor("ps_c", [128, 512], F32))

    sems = {}
    cnt = {}
    for name in ["dL0", "dL1", "dL2", "dL3", "dT0", "dT1", "dC", "dO",
                 "cV", "cS", "cT", "cG"]:
        sems[name] = es_glob.enter_context(nc.semaphore(name))
        cnt[name] = 0

    state = {"eng": None, "h": None}

    class _Null:
        def then_inc(self, *a, **k):
            return self
    NULL = _Null()

    def on(eng):
        return state["eng"] == eng

    def emit(eng, fn):
        if on(eng):
            return fn(state["h"])
        return NULL

    def inc(ins, name, by=1):
        cnt[name] += by
        ins.then_inc(sems[name], by)
        return cnt[name]

    def wait(eng, name, val):
        if on(eng) and val > 0:
            state["h"].wait_ge(sems[name], val)

    def drain(eng):
        if on(eng):
            state["h"].drain()

    rr = {"dL": 0, "dT": 0}
    POOLS = {"dL": ["dL0", "dL1", "dL2", "dL3"], "dT": ["dT0", "dT1"]}

    def dma(pool_name, fn, eng="y"):
        sname = POOLS[pool_name][rr[pool_name] % len(POOLS[pool_name])]
        rr[pool_name] += 1
        wait(eng, sname, cnt[sname])
        ins = emit(eng, fn)
        val = inc(ins, sname, 16)
        return (sname, val)

    def waitt(eng, tok):
        wait(eng, tok[0], tok[1])

    def reset_rr():
        rr["dL"] = 0
        rr["dT"] = 0

    def wait_pool(eng, pool_name):
        for sname in POOLS[pool_name]:
            wait(eng, sname, cnt[sname])

    v = lambda fn: emit("v", fn)
    sc = lambda fn: emit("s", fn)
    pe = lambda fn: emit("t", fn)
    gp = lambda fn: emit("g", fn)
    sy = lambda fn: emit("y", fn)

    def program_a(T):
        xnownT, kT, vb, qTb, aoT, U, ext, mask, smask, rs2, rs3, kqc = (
            T["xnownT"], T["kT"], T["vb"], T["qTb"], T["aoT"], T["U"],
            T["ext"], T["mask"], T["smask"], T["rs2"], T["rs3"], T["kqc"])
        ebuf = U[:, 0:WMAX]
        pT3 = lambda nblk: U[:, WMAX:WMAX + 128 * nblk].rearrange(
            "p (a b) -> p a b", b=128)
        pTs = lambda blk: U[:, WMAX + 128 * blk:WMAX + 128 * (blk + 1)]
        wob_sl = lambda hh, a, b: U[:, 2048 * hh + a:2048 * hh + b]

        inc(sy(lambda h: h.dma_start(out=biasq[:, :], in_=bqkT.ap())), "dC", 16)
        inc(sy(lambda h: h.dma_start(
            out=pkcol[:, :, :].rearrange("p a b -> p (a b)"),
            in_=pkT_h.ap())), "dC", 16)
        inc(sy(lambda h: h.dma_start(out=gscol[:, :], in_=gsT_h.ap())), "dC", 16)
        inc(sy(lambda h: h.dma_start(out=uscol[:, :], in_=usT_h.ap())), "dC", 16)
        inc(sy(lambda h: h.dma_start(out=qoff[:, :], in_=qoff_h.ap())), "dC", 16)
        small_ld = cnt["dC"]

        ig = gp(lambda h: h.iota(mask[:, :], pattern=[[1, 260]], base=0,
                                 channel_multiplier=-1,
                                 allow_small_or_imprecise_dtypes=True))
        g_iota = inc(ig, "cG")

        wait("v", "dC", small_ld)
        wait("v", "cG", g_iota)
        v(lambda h: h.tensor_scalar(out=mask[:, :], in0=mask[:, :], scalar1=qoff[:, :],
                                    scalar2=None, op0=ALU.subtract))
        drain("v")
        v(lambda h: h.tensor_scalar(out=mask[:, :], in0=mask[:, :], scalar1=4.0,
                                    scalar2=NEG, op0=ALU.is_gt, op1=ALU.mult))
        drain("v")
        v(lambda h: h.memset(brow[0:1, 0:128], 1.0))
        iv = v(lambda h: h.tensor_scalar(out=biasq[:, 0:NT], in0=biasq[:, 0:NT],
                                         scalar1=SCALE, scalar2=None, op0=ALU.mult))
        const_done = inc(iv, "cV")

        last_sq = 0
        for t in range(NT + NOWN):
            if t == 0:
                wait("y", "cV", const_done)
            if t < NT:
                tok = dma("dL", lambda h, t=t: h.dma_start(
                    out=pool[:, :], in_=xfull.ap()[128 * t:128 * (t + 1), :]))
            else:
                tok = dma("dL", lambda h, t=t: h.dma_start(
                    out=acc[:, t - NT, :],
                    in_=xown.ap()[128 * (t - NT):128 * (t - NT + 1), :]))
            waitt("v", tok)
            waitt("s", tok)
            srcap = (lambda t: pool[:, :] if t < NT else acc[:, t - NT, :])(t)
            iv = v(lambda h, t=t, srcap=srcap: h.reduce_sum(
                stats[:, 0, t:t + 1], srcap, axis=AX.X))
            rs_v = inc(iv, "cV")
            isq = sc(lambda h, t=t, srcap=srcap: h.activation(
                out=ps_s[:, 0:E], in_=srcap, func=AF.Square,
                accum_out=stats[:, 1, t:t + 1]))
            last_sq = inc(isq, "cS")
            drain("s")
            if t < NT - 1:
                wait("y", "cS", last_sq)
                wait("y", "cV", rs_v)
        wait("v", "cS", last_sq)
        drain("v")
        v(lambda h: h.tensor_scalar(out=stats[:, 2, :], in0=stats[:, 0, :],
                                    scalar1=1.0 / E, scalar2=None, op0=ALU.mult))
        v(lambda h: h.tensor_scalar(out=stats[:, 3, :], in0=stats[:, 1, :],
                                    scalar1=1.0 / E, scalar2=None, op0=ALU.mult))
        drain("v")
        v(lambda h: h.tensor_tensor(out=stats[:, 4, :], in0=stats[:, 2, :],
                                    in1=stats[:, 2, :], op=ALU.mult))
        drain("v")
        v(lambda h: h.tensor_tensor(out=stats[:, 3, :], in0=stats[:, 3, :],
                                    in1=stats[:, 4, :], op=ALU.subtract))
        drain("v")
        iv = v(lambda h: h.tensor_scalar(out=stats[:, 3, :], in0=stats[:, 3, :],
                                         scalar1=EPS, scalar2=None, op0=ALU.add))
        st_v = inc(iv, "cV")
        wait("s", "cV", st_v)
        isq = sc(lambda h: h.activation(out=stats[:, 4, :], in_=stats[:, 3, :],
                                        func=AF.Sqrt))
        st_s = inc(isq, "cS")
        wait("v", "cS", st_s)
        iv = v(lambda h: h.reciprocal(stats[:, 5, :], stats[:, 4, :]))
        drain("v")
        stats_done = inc(iv, "cV")

        prev_tr = None
        wait("y", "cV", stats_done)
        wait("y", "cS", st_s)
        for t in range(NT + NOWN):
            if t < NT:
                tok = dma("dL", lambda h, t=t: h.dma_start(
                    out=pool[:, :], in_=xfull.ap()[128 * t:128 * (t + 1), :]))
                waitt("v", tok)
            wait("v", "cV", stats_done)
            if prev_tr is not None:
                waitt("v", prev_tr)
            srcap = (lambda t: pool[:, :] if t < NT else acc[:, t - NT, :])(t)
            iv = v(lambda h, t=t, srcap=srcap: h.scalar_tensor_tensor(
                out=wbf, in0=srcap, scalar=stats[:, 2, t:t + 1],
                in1=stats[:, 5, t:t + 1].to_broadcast([128, E]),
                op0=ALU.subtract, op1=ALU.mult))
            nv = inc(iv, "cV")
            wait("y", "cV", nv)
            if t < NT:
                prev_tr = dma("dT", lambda h, t=t: h.dma_start_transpose(
                    out=big[:, :, 128 * t:128 * (t + 1)], in_=wbf))
            else:
                prev_tr = dma("dT", lambda h, t=t: h.dma_start_transpose(
                    out=xnownT[:, :, 128 * (t - NT):128 * (t - NT + 1)], in_=wbf))

        xnb_first = [prev_tr]
        att_state = {"expO_prev": 0, "next": None, "av_prev": 0, "psa": 0,
                     "sel_prev": 0, "mv_prev": 0,
                     "pool_cv": cnt["cV"], "pool_cs": 0}
        es_tok = [None, None, None, None]
        wo_mm_hist = [0]
        kq_used = [0, 0]
        p1_state = {"kT_g": 0, "qready": 0}

        def p1_kq(ct):
            wait("s", "cV", att_state["pool_cv"])
            wait("s", "cS", att_state["pool_cs"])
            for _sl in (0, 1):
                if es_tok[_sl] is not None:
                    for _t in es_tok[_sl]:
                        waitt("s", _t)
            tok = dma("dL", lambda h, ct=ct: h.dma_start(
                out=pool[:, :].rearrange("p (a c) -> p a c", c=128),
                in_=Wk_h.ap()[:, 128 * ct:128 * (ct + 1)].rearrange(
                    "(a p) c -> p a c", p=128)), eng="s")
            last_mm = NULL
            kc_last = 0
            for c in range(4):
                if c == 0:
                    waitt("v", tok)
                wait("v", "cT", kq_used[c % 2])
                iv = v(lambda h, c=c: h.tensor_copy(
                    kqc[:, 512 * (c % 2):512 * (c % 2) + 512],
                    pool[:, 512 * c:512 * (c + 1)]))
                kc_last = inc(iv, "cV")
                wait("t", "cV", kc_last)
                if ct == 0 and c == 0:
                    wait_pool("t", "dT")
                if c == 0:
                    wait("t", "cS", cnt["cS"])
                for fi in range(4):
                    for ch in range(4):
                        last_mm = pe(lambda h, c=c, fi=fi, ch=ch: h.matmul(
                            out=ps_s[:, 512 * ch:512 * (ch + 1)],
                            lhsT=kqc[:, 512 * (c % 2) + 128 * fi:
                                     512 * (c % 2) + 128 * (fi + 1)],
                            rhs=big[:, 4 * c + fi, 512 * ch:512 * (ch + 1)],
                            start=(c == 0 and fi == 0),
                            stop=(c == 3 and fi == 3)))
                kq_used[c % 2] = inc(last_mm, "cT")
            prev_pe = kq_used[1]
            wait("s", "cT", prev_pe)
            isc = sc(lambda h, ct=ct: h.activation(
                out=kT[:, 0:2048], in_=ps_s[:, 0:2048], func=AF.Identity,
                bias=biasq[:, NT + ct:NT + ct + 1], scale=1.0))
            sv = inc(isc, "cS")
            wait("t", "cS", sv)
            wait("g", "cS", sv)
            ig = gp(lambda h, ct=ct: h.tensor_copy(
                kT[:, 2048:2048 + P], pkcol[:, ct, :]))
            p1_state["kT_g"] = inc(ig, "cG")

            wait("s", "cV", kc_last)
            tok = dma("dL", lambda h, ct=ct: h.dma_start(
                out=pool[:, :].rearrange("p (a c) -> p a c", c=128),
                in_=Wq_h.ap()[:, 128 * ct:128 * (ct + 1)].rearrange(
                    "(a p) c -> p a c", p=128)), eng="s")
            last_mm = NULL
            qc_last = 0
            for c in range(4):
                if c == 0:
                    waitt("v", tok)
                wait("v", "cT", kq_used[c % 2])
                iv = v(lambda h, c=c: h.tensor_copy(
                    kqc[:, 512 * (c % 2):512 * (c % 2) + 512],
                    pool[:, 512 * c:512 * (c + 1)]))
                qc_last = inc(iv, "cV")
                wait("t", "cV", qc_last)
                if c == 0:
                    wait("t", "cS", att_state["psa"])
                for fi in range(4):
                    for half in range(2):
                        tgt = ps_a if half == 0 else ps_b
                        last_mm = pe(lambda h, c=c, fi=fi, half=half, tgt=tgt:
                                     h.matmul(
                            out=tgt[:, :],
                            lhsT=kqc[:, 512 * (c % 2) + 128 * fi:
                                     512 * (c % 2) + 128 * (fi + 1)],
                            rhs=xnownT[:, 4 * c + fi, 512 * half:512 * (half + 1)],
                            start=(c == 0 and fi == 0),
                            stop=(c == 3 and fi == 3)))
                kq_used[c % 2] = inc(last_mm, "cT")
            prev_pe = kq_used[1]
            sv = 0
            for half in range(2):
                tgt = ps_a if half == 0 else ps_b
                wait("s", "cT", prev_pe)
                isc = sc(lambda h, ct=ct, half=half, tgt=tgt: h.activation(
                    out=qTb[:, 512 * half:512 * (half + 1)], in_=tgt[:, :],
                    func=AF.Identity, bias=biasq[:, ct:ct + 1], scale=SCALE))
                sv = inc(isc, "cS")
                wait("t", "cS", sv)
            p1_state["qready"] = sv
            att_state["pool_cv"] = qc_last

        for g in range(HEADS):
            ct = g
            c0 = 128 * g
            if g == 0:
                p1_kq(0)

            wait("y", "cV", att_state["pool_cv"])
            tok = dma("dL", lambda h, c0=c0: h.dma_start(
                out=pool[:, :].rearrange("p (a c) -> p a c", c=128),
                in_=Wv_h.ap()[:, c0:c0 + 128].rearrange("(a p) c -> p a c", p=128)))
            waitt("v", tok)
            wait("v", "cT", cnt["cT"])
            iv = v(lambda h: h.tensor_copy(U[:, 0:2048], pool[:, :]))
            wv_done = inc(iv, "cV")
            wait("y", "cV", wv_done)
            tok = dma("dL", lambda h, c0=c0: h.dma_start(
                out=pool[64:64 + P, 0:128], in_=pv_h.ap()[:, c0:c0 + 128]))
            waitt("g", tok)
            wait("g", "cT", cnt["cT"])
            gp(lambda h: h.memset(vb[:, 16, :], 0.0))
            drain("g")
            ig = gp(lambda h: h.tensor_copy(vb[0:P, 16, :], pool[64:64 + P, 0:128]))
            pv_g = inc(ig, "cG")
            tokb = dma("dL", lambda h, c0=c0: h.dma_start(
                out=pool[96:97, 0:128], in_=bv_r.ap()[0:1, c0:c0 + 128]))
            waitt("v", tokb)
            wait("v", "cT", cnt["cT"])
            iv = v(lambda h: h.tensor_copy(brow[0:1, 128:256], pool[96:97, 0:128]))
            bv_cv = inc(iv, "cV")

            ev_hist = []
            for tt in range(NT):
                if tt == 0:
                    wait("t", "cV", bv_cv)
                vtgt = ps_c if tt % 2 == 0 else ps_b
                if len(ev_hist) >= 2:
                    wait("t", "cV", ev_hist[-2])
                last_mm = NULL
                for ft in range(NT):
                    last_mm = pe(lambda h, tt=tt, ft=ft, vtgt=vtgt: h.matmul(
                        out=vtgt[:, 0:128], lhsT=big[:, ft, 128 * tt:128 * (tt + 1)],
                        rhs=U[:, 128 * ft:128 * (ft + 1)],
                        start=(ft == 0), stop=False))
                last_mm = pe(lambda h, vtgt=vtgt: h.matmul(
                    out=vtgt[:, 0:128], lhsT=brow[0:1, 0:128],
                    rhs=brow[0:1, 128:256], start=False, stop=True))
                tv = inc(last_mm, "cT")
                wait("v", "cT", tv)
                iv = v(lambda h, tt=tt, vtgt=vtgt: h.tensor_copy(
                    vb[:, tt, :], vtgt[:, 0:128]))
                ev_hist.append(inc(iv, "cV"))
            v_done = cnt["cV"]

            poolb = pool[:, :].bitcast(BF16)
            RSArr = [rs2[:, 0:1796], smask[:, 0:2052], rs3[:, 0:1540]]
            ES_map = [0, 1, 2, 3, 0, 1, 2, 3]
            ES = [poolb[:, 0:1536], poolb[:, 1536:3200],
                  wcast[:, 0:1920], ebuf[:, 0:2176]]

            def tW(t):
                return wj(t) - 124

            def RS(t):
                return RSArr[t % 3]

            def emit_scores(t, gate_cs, gate_cv):
                W = tW(t)
                wait("t", "cS", max(gate_cs, p1_state["qready"]))
                wait("t", "cV", max(v_done, gate_cv))
                wait("t", "cG", p1_state["kT_g"])
                off = 0
                lm = NULL
                while off < W:
                    n = min(512, W - off)
                    lm = pe(lambda h, t=t, off=off, n=n: h.matmul(
                        out=ps_s[:, off:off + n],
                        lhsT=qTb[:, 128 * t:128 * (t + 1)],
                        rhs=kT[:, off:off + n], start=True, stop=True))
                    off += n
                return inc(lm, "cT")

            def prep_ops(t, sct, par):
                W = tW(t)
                nb = wj(t) // 128
                ec = 0 if par == 0 else 64
                cc = 0 if par == 0 else 4
                ops = []
                ops.append((lambda h, W=W: h.tensor_tensor(
                    out=ps_s[:, W - 260:W], in0=ps_s[:, W - 260:W],
                    in1=mask[:, 0:260], op=ALU.add),
                    [("cT", sct)], "mask"))
                ops.append((lambda h, W=W, ec=ec: h.max(
                    out=ext[:, ec:ec + 8], in_=ps_s[:, 0:W]), None, "max8"))
                ops.append((lambda h, ec=ec, cc=cc: h.tensor_scalar(
                    out=cols[:, cc:cc + 1], in0=ext[:, ec:ec + 1], scalar1=-1.0,
                    scalar2=None, op0=ALU.mult), None, None))
                es = ES[ES_map[t]]
                mg = list(es_tok[ES_map[t]] or [])
                ops.append((lambda h, W=W, nb=nb, es=es: h.memset(
                    es[:, W:128 * nb], 0.0), mg or None, None))
                return ops

            def emit_copy(t, max8_cv, rs_free_cv):
                W = tW(t)
                rs = RS(t)
                wait("s", "cV", max(max8_cv, rs_free_cv))
                isc = sc(lambda h, W=W, rs=rs: h.activation(
                    out=rs[:, 0:W], in_=ps_s[:, 0:W], func=AF.Identity))
                return inc(isc, "cS")

            def emit_exp(t, par, mv_val):
                W = tW(t)
                cc = 0 if par == 0 else 4
                es = ES[ES_map[t]]
                tok3 = es_tok[ES_map[t]]
                if tok3 is not None:
                    for _t in tok3:
                        waitt("s", _t)
                if ES_map[t] == 2 and xnb_first[0] is not None:
                    waitt("s", xnb_first[0])
                    xnb_first[0] = None
                if ES_map[t] in (0, 1):
                    wait("s", "cG", pv_g)
                wait("s", "cV", mv_val)
                isc = sc(lambda h, W=W, cc=cc, es=es: h.activation(
                    out=es[:, 0:W], in_=ps_s[:, 0:W], func=AF.Exp,
                    bias=cols[:, cc:cc + 1], scale=1.0))
                return inc(isc, "cS")

            def emit_prep_inline(t, sct, par, rs_free_cv):
                ops = prep_ops(t, sct, par)
                vals = []
                for fn, gates, _tag in ops:
                    for gate in (gates or []):
                        wait("v", gate[0], gate[1])
                    last = v(fn)
                    drain("v")
                    vals.append(inc(last, "cV"))
                cp = emit_copy(t, vals[1], rs_free_cv)
                return vals[-1], cp

            stA = att_state
            for p in range(NOWN // 2):
                jE, jO = 2 * p, 2 * p + 1
                WE, WO = tW(jE), tW(jO)
                nbE, nbO = wj(jE) // 128, wj(jO) // 128
                if p == 0:
                    scE = emit_scores(jE, stA["expO_prev"], stA["mv_prev"])
                    mvE, cpE = emit_prep_inline(jE, scE, 0, stA["sel_prev"])
                    evE = emit_exp(jE, 0, mvE)
                    scO = emit_scores(jO, evE, mvE)
                    mvO, cpO = emit_prep_inline(jO, scO, 1, stA["sel_prev"])
                    evO = emit_exp(jO, 1, mvO)
                else:
                    mvE, cpE, evE, mvO, cpO, evO = stA["next"]
                if p + 1 < NOWN // 2:
                    jE2, jO2 = 2 * p + 2, 2 * p + 3
                    if stA.get("scE_pre") and stA["scE_pre"][0] == jE2:
                        scE2 = stA["scE_pre"][1]
                    else:
                        scE2 = emit_scores(jE2, evO, mvO)
                    g1 = prep_ops(jE2, scE2, 0)
                else:
                    jE2 = jO2 = None
                    g1 = []
                wait("v", "cS", cpE)
                wait("v", "cS", cpO)
                lastE, lastO = 0, 0
                gq = list(g1)
                mvE2 = [None]
                cpE2 = [None]
                mkO2 = [None]
                for r in range(8):
                    if r > 0:
                        wait("v", "cV", lastE)
                        iv = v(lambda h, r=r, WE=WE, jE=jE: h.max(
                            out=ext[:, 8 * r:8 * (r + 1)], in_=RS(jE)[:, 0:WE]))
                        lastE = inc(iv, "cV")
                        wait("v", "cV", lastO)
                        iv = v(lambda h, r=r, WO=WO, jO=jO: h.max(
                            out=ext[:, 64 + 8 * r:64 + 8 * (r + 1)],
                            in_=RS(jO)[:, 0:WO]))
                        lastO = inc(iv, "cV")
                    else:
                        wait("v", "cV", mvE)
                        wait("v", "cV", mvO)
                    wait("v", "cV", lastE)
                    iv = v(lambda h, r=r, WE=WE, jE=jE: h.match_replace(
                        out=RS(jE)[:, 0:WE], in_to_replace=ext[:, 8 * r:8 * (r + 1)],
                        in_values=RS(jE)[:, 0:WE], imm_value=NEG))
                    lastE = inc(iv, "cV")
                    wait("v", "cV", lastO)
                    iv = v(lambda h, r=r, WO=WO, jO=jO: h.match_replace(
                        out=RS(jO)[:, 0:WO],
                        in_to_replace=ext[:, 64 + 8 * r:64 + 8 * (r + 1)],
                        in_values=RS(jO)[:, 0:WO], imm_value=NEG))
                    lastO = inc(iv, "cV")
                    if gq:
                        fn, gates, tag = gq.pop(0)
                        for gate in (gates or []):
                            wait("v", gate[0], gate[1])
                        wait("v", "cV", lastO)
                        gi = v(fn)
                        drain("v")
                        gv = inc(gi, "cV")
                        if tag == "max8":
                            if mkO2[0] is None and cpE2[0] is not None:
                                mkO2[0] = gv
                            elif cpE2[0] is None:
                                cpE2[0] = emit_copy(jE2, gv, stA["sel_prev"])
                        if not gq and mvE2[0] is None and jE2 is not None:
                            mvE2[0] = gv
                    if r == 5 and jE2 is not None:
                        evE2 = emit_exp(jE2, 0, mvE2[0])
                        scO2 = emit_scores(jO2, evE2, mvE2[0])
                        gq.extend(prep_ops(jO2, scO2, 1))
                wait("v", "cV", lastE)
                wait("v", "cS", evE)
                iv = v(lambda h, WE=WE, jE=jE: h.scalar_tensor_tensor(
                    out=ES[ES_map[jE]][:, 0:WE], in0=RS(jE)[:, 0:WE], scalar=NEGT,
                    in1=ES[ES_map[jE]][:, 0:WE], op0=ALU.is_lt, op1=ALU.mult,
                    accum_out=cols[:, 1:2]))
                pvE = inc(iv, "cV")
                drain("v")
                mvO2 = None
                for fn, gates, tag in gq:
                    for gate in (gates or []):
                        wait("v", gate[0], gate[1])
                    gi = v(fn)
                    drain("v")
                    mvO2 = inc(gi, "cV")
                    if tag == "max8" and mkO2[0] is None:
                        mkO2[0] = mvO2
                if jE2 is not None:
                    cpO2 = emit_copy(jO2, mkO2[0], pvE)
                    evO2 = emit_exp(jO2, 1, mvO2)
                    if jO2 + 1 <= 2 * (NOWN // 2 - 1):
                        stA["scE_pre"] = (jO2 + 1,
                                          emit_scores(jO2 + 1, evO2, mvO2))
                    else:
                        stA["scE_pre"] = None
                    stA["next"] = (mvE2[0], cpE2[0], evE2, mvO2, cpO2, evO2)
                    stA["expO_prev"] = evO2
                    stA["mv_prev"] = mvO2
                else:
                    stA["mv_prev"] = mvO
                wait("v", "cV", lastO)
                wait("v", "cS", evO)
                iv = v(lambda h, WO=WO, jO=jO: h.scalar_tensor_tensor(
                    out=ES[ES_map[jO]][:, 0:WO], in0=RS(jO)[:, 0:WO], scalar=NEGT,
                    in1=ES[ES_map[jO]][:, 0:WO], op0=ALU.is_lt, op1=ALU.mult,
                    accum_out=cols[:, 5:6]))
                pvO = inc(iv, "cV")
                stA["sel_prev"] = pvO
                drain("v")
                wait("v", "cS", stA["psa"])
                v(lambda h: h.reciprocal(cols[:, 2:3], cols[:, 1:2]))
                iv = v(lambda h: h.reciprocal(cols[:, 6:7], cols[:, 5:6]))
                rcp = inc(iv, "cV")
                pre_esc = {}
                if jE2 is None and g + 1 < HEADS:
                    for (jT, Wt, par) in ((jE, WE, 0), (jO, WO, 1)):
                        es = ES[ES_map[jT]]
                        cc = 2 if par == 0 else 6
                        wait("s", "cV", rcp)
                        isc = sc(lambda h, Wt=Wt, es=es, cc=cc: h.activation(
                            out=es[:, 0:Wt], in_=es[:, 0:Wt], func=AF.Identity,
                            scale=cols[:, cc:cc + 1]))
                        pre_esc[jT] = inc(isc, "cS")
                    p1_kq(g + 1)
                for (jT, Wt, nbt, par) in ((jE, WE, nbE, 0), (jO, WO, nbO, 1)):
                    es = ES[ES_map[jT]]
                    cc = 2 if par == 0 else 6
                    if jT in pre_esc:
                        esc = pre_esc[jT]
                    else:
                        wait("s", "cV", rcp)
                        isc = sc(lambda h, Wt=Wt, es=es, cc=cc: h.activation(
                            out=es[:, 0:Wt], in_=es[:, 0:Wt], func=AF.Identity,
                            scale=cols[:, cc:cc + 1]))
                        esc = inc(isc, "cS")
                    wait("y", "cS", esc)
                    wait("y", "cT", stA["av_prev"])
                    nb0 = (nbt + 1) // 2
                    t1 = dma("dT", lambda h, nb0=nb0, es=es: h.dma_start_transpose(
                        out=pT3(nb0), in_=es[:, 0:128 * nb0]))
                    t2 = dma("dT", lambda h, nb0=nb0, nbt=nbt, es=es:
                             h.dma_start_transpose(
                        out=U[:, WMAX + 128 * nb0:WMAX + 128 * nbt].rearrange(
                            "p (a b) -> p a b", b=128),
                        in_=es[:, 128 * nb0:128 * nbt]))
                    es_tok[ES_map[jT]] = [t1, t2]
                    waitt("t", t1)
                    waitt("t", t2)
                    wait("t", "cS", stA["psa"])
                    wait("t", "cG", pv_g)
                    nfull = min(nbt, 16)
                    lm = NULL
                    for blk in range(nfull):
                        lm = pe(lambda h, blk=blk, nfull=nfull, nbt=nbt: h.matmul(
                            out=ps_a[:, 0:128], lhsT=vb[:, blk, :],
                            rhs=pTs(blk), start=(blk == 0),
                            stop=(blk == nfull - 1 and nbt == nfull)))
                    if nbt > 16:
                        lm = pe(lambda h: h.matmul(
                            out=ps_a[:, 0:128], lhsT=vb[0:P, 16, :],
                            rhs=pTs(16)[0:P, :], start=False, stop=True))
                    av = inc(lm, "cT")
                    stA["av_prev"] = av
                    wait("s", "cT", av)
                    isc = sc(lambda h, g=g, jT=jT: h.activation(
                        out=aoT[:, g % 2, 128 * jT:128 * (jT + 1)],
                        in_=ps_a[:, 0:128], func=AF.Identity))
                    stA["psa"] = inc(isc, "cS")
                if p in (0, 2):
                    stA["pool_cv"] = pvO
                    stA["pool_cs"] = stA["psa"]

            if g % 2 == 1:
                wo_cp = 0
                for hh in range(2):
                    ct2 = g - 1 + hh
                    if hh == 0:
                        wait("y", "cV", att_state["pool_cv"])
                        wait("y", "cS", att_state["pool_cs"])
                    else:
                        wait("y", "cV", wo_cp)
                    for _sl in (0, 1):
                        if es_tok[_sl] is not None:
                            for _t in es_tok[_sl]:
                                waitt("y", _t)
                    tok = dma("dL", lambda h, ct2=ct2: h.dma_start(
                        out=pool[:, :], in_=Wo_h.ap()[128 * ct2:128 * (ct2 + 1), :]))
                    waitt("v", tok)
                    wait("v", "cT", cnt["cT"])
                    iv = v(lambda h, hh=hh: h.tensor_copy(
                        wob_sl(hh, 0, 2048), pool[:, :]))
                    wo_cp = inc(iv, "cV")
                if g == 1:
                    wait("y", "cV", wo_cp)
                    tokb = dma("dL", lambda h: h.dma_start(
                        out=pool[96:97, :], in_=bo_r.ap()))
                    waitt("v", tokb)
                    wait("v", "cV", att_state["sel_prev"])
                    iv = v(lambda h: h.tensor_copy(
                        smask[:, :].bitcast(BF16)[0:1, 0:2048], pool[96:97, :]))
                    wo_cp = inc(iv, "cV")
                wo_ld = cnt["cV"]
                wait("t", "cV", wo_ld)
                wait("t", "cS", cnt["cS"])
                add_hist = []
                for it_i, (tt, ec) in enumerate([(a, b) for a in range(NOWN)
                                                 for b in range(4)]):
                    tgt = ps_b if it_i % 2 == 0 else ps_c
                    if len(add_hist) >= 2:
                        wait("t", "cV", add_hist[-2])
                    last_mm = NULL
                    for hh in range(2):
                        last_mm = pe(lambda h, hh=hh, tt=tt, ec=ec, tgt=tgt: h.matmul(
                            out=tgt[:, :], lhsT=aoT[:, hh, 128 * tt:128 * (tt + 1)],
                            rhs=wob_sl(hh, 512 * ec, 512 * (ec + 1)),
                            start=(hh == 0), stop=(hh == 1 and g != 1)))
                    if g == 1:
                        last_mm = pe(lambda h, ec=ec, tgt=tgt: h.matmul(
                            out=tgt[:, :], lhsT=brow[0:1, 0:128],
                            rhs=smask[:, :].bitcast(BF16)[
                                0:1, 512 * ec:512 * (ec + 1)],
                            start=False, stop=True))
                    wt = inc(last_mm, "cT")
                    wait("v", "cT", wt)
                    iv = v(lambda h, tt=tt, ec=ec, tgt=tgt: h.tensor_tensor(
                        out=acc[:, tt, 512 * ec:512 * (ec + 1)],
                        in0=acc[:, tt, 512 * ec:512 * (ec + 1)],
                        in1=tgt[:, :], op=ALU.add))
                    add_hist.append(inc(iv, "cV"))
                wo_mm_hist[0] = cnt["cT"]
                att_state["pool_cv"] = max(att_state["pool_cv"], wo_cp)

    def program_b(T, barrier):
        fraw, fbf, fbf2, wgT, wdT, gbuf, sexp, onesB = (
            T["fraw"], T["fbf"], T["fbf2"], T["wgT"], T["wdT"], T["gbuf"],
            T["sexp"], T["onesB"])
        bar_v, bar_s, bar_t = barrier

        for e in ("y", "v", "s", "t", "g"):
            wait(e, "cV", bar_v)
            wait(e, "cS", bar_s)
            wait(e, "cT", bar_t)

        last_sq = 0
        for t in range(NOWN):
            v(lambda h, t=t: h.reduce_sum(stats[:, 0, t:t + 1], acc[:, t, :], axis=AX.X))
            isq = sc(lambda h, t=t: h.activation(
                out=ps_s[:, 0:E], in_=acc[:, t, :], func=AF.Square,
                accum_out=stats[:, 1, t:t + 1]))
            last_sq = inc(isq, "cS")
            drain("s")
        wait("v", "cS", last_sq)
        drain("v")
        v(lambda h: h.tensor_scalar(out=stats[:, 2, 0:NOWN], in0=stats[:, 0, 0:NOWN],
                                    scalar1=1.0 / E, scalar2=None, op0=ALU.mult))
        v(lambda h: h.tensor_scalar(out=stats[:, 3, 0:NOWN], in0=stats[:, 1, 0:NOWN],
                                    scalar1=1.0 / E, scalar2=None, op0=ALU.mult))
        drain("v")
        v(lambda h: h.tensor_tensor(out=stats[:, 4, 0:NOWN], in0=stats[:, 2, 0:NOWN],
                                    in1=stats[:, 2, 0:NOWN], op=ALU.mult))
        drain("v")
        v(lambda h: h.tensor_tensor(out=stats[:, 3, 0:NOWN], in0=stats[:, 3, 0:NOWN],
                                    in1=stats[:, 4, 0:NOWN], op=ALU.subtract))
        drain("v")
        iv = v(lambda h: h.tensor_scalar(out=stats[:, 3, 0:NOWN], in0=stats[:, 3, 0:NOWN],
                                         scalar1=EPS, scalar2=None, op0=ALU.add))
        st_v = inc(iv, "cV")
        wait("s", "cV", st_v)
        isq = sc(lambda h: h.activation(out=stats[:, 4, 0:NOWN], in_=stats[:, 3, 0:NOWN],
                                        func=AF.Sqrt))
        st_s = inc(isq, "cS")
        wait("v", "cS", st_s)
        iv = v(lambda h: h.reciprocal(stats[:, 5, 0:NOWN], stats[:, 4, 0:NOWN]))
        drain("v")
        stats2_done = inc(iv, "cV")

        prev_tr = None
        for t in range(NOWN):
            wait("v", "cV", stats2_done)
            if prev_tr is not None:
                waitt("v", prev_tr)
            iv = v(lambda h, t=t: h.scalar_tensor_tensor(
                out=wbf, in0=acc[:, t, :], scalar=stats[:, 2, t:t + 1],
                in1=stats[:, 5, t:t + 1].to_broadcast([128, E]),
                op0=ALU.subtract, op1=ALU.mult))
            nv = inc(iv, "cV")
            wait("y", "cV", nv)
            prev_tr = dma("dT", lambda h, t=t: h.dma_start_transpose(
                out=big[:, :, 128 * t:128 * (t + 1)], in_=wbf))

        tok = dma("dL", lambda h: h.dma_start(out=pool[0:1, :], in_=dsr_h.ap()))
        iv = v(lambda h: h.memset(onesB[0:1, :], 1.0))
        ones_cv = inc(iv, "cV")
        waitt("t", tok)
        wait("t", "cV", ones_cv)
        sx_t = 0
        sx_cv = 0
        for ec in range(4):
            wait("t", "cV", sx_cv)
            lm = pe(lambda h, ec=ec: h.matmul(
                out=ps_b[:, :], lhsT=onesB[0:1, :],
                rhs=pool[0:1, 512 * ec:512 * (ec + 1)], start=True, stop=True))
            sx_t = inc(lm, "cT")
            wait("v", "cT", sx_t)
            iv = v(lambda h, ec=ec: h.tensor_copy(
                sexp[:, 512 * ec:512 * (ec + 1)], ps_b[:, :]))
            sx_cv = inc(iv, "cV")
        prev_dn_t = 0
        fslots = [fraw[:, :], pool[:, :].bitcast(I32)]
        fconsumed = [0, sx_cv]
        bslots = [fbf[:, :], fbf2[:, :]]
        bguard = [None, None]
        wg_used = [0, 0, 0, 0]
        wg_tok = [None, None, None, None]
        hmul_hist = [0, 0]
        last_down_add = [0]
        silu_last = [0]
        gu_steps = []
        for blk in range(N_IT // DN_BLK):
            for itl in range(DN_BLK):
                it_g = blk * DN_BLK + itl
                gu_steps.append((it_g, 0, gw_h))
                gu_steps.append((it_g, 1, uw_h))
        gu_pre = [None]

        def gu_load(step_idx):
            it_g, which, w_h = gu_steps[step_idx]
            slot = step_idx % 2
            wait("y", "cV", fconsumed[slot])
            return dma("dL", lambda h, it_g=it_g, w_h=w_h, slot=slot: h.dma_start(
                out=fslots[slot], in_=w_h.ap()[128 * it_g:128 * (it_g + 1), :]))

        def gu_prep(step_idx, tok):
            it_g, which, w_h = gu_steps[step_idx]
            scol = gscol if which == 0 else uscol
            slot = step_idx % 2
            bslot = step_idx % 2
            wslot = 2 * (it_g % 2) + which
            waitt("v", tok)
            if bguard[bslot] is not None:
                for _g in bguard[bslot]:
                    waitt("v", _g)
            iv = v(lambda h, it_g=it_g, scol=scol, slot=slot, bslot=bslot:
                   h.scalar_tensor_tensor(
                out=bslots[bslot], in0=fslots[slot], scalar=128.0,
                in1=scol[:, it_g:it_g + 1].to_broadcast([128, E]),
                op0=ALU.subtract, op1=ALU.mult))
            cv = inc(iv, "cV")
            fconsumed[slot] = cv
            wait("s", "cV", cv)
            wait("s", "cT", wg_used[wslot])
            toks = []
            for hf in range(2):
                toks.append(dma("dT", lambda h, wslot=wslot, bslot=bslot, hf=hf:
                                h.dma_start_transpose(
                    out=wgT[:, wslot, 8 * hf:8 * (hf + 1), :],
                    in_=bslots[bslot][:, 1024 * hf:1024 * (hf + 1)]), eng="s"))
            bguard[bslot] = list(toks)
            return (wslot, toks)

        gu_pre[0] = gu_load(0)
        prepped = []
        for w in range(2):
            tok = gu_pre[0]
            gu_pre[0] = gu_load(w + 1)
            prepped.append(gu_prep(w, tok))
        for blk in range(N_IT // DN_BLK):
            pend = [None]
            for itl in range(DN_BLK):
                it_g = blk * DN_BLK + itl
                wtk = prepped
                prepped = []
                for wslot, toks in wtk:
                    for tk in toks:
                        waitt("t", tk)
                if blk == 0 and itl == 0:
                    wait_pool("t", "dT")
                psets = ([ps_a[:, 0:512], ps_b[:, 0:512],
                          ps_c[:, 0:512], ps_s[:, 0:512]],
                         [ps_s[:, 512:1024], ps_s[:, 1024:1536],
                          ps_s[:, 1536:2048], ps_a[:, 0:512]])
                p_par = it_g % 2
                pset = psets[p_par]
                wait("t", "cV", max(hmul_hist[-3] if len(hmul_hist) >= 3 else 0,
                                    last_down_add[0]))
                last_mm = NULL
                shared_slot = 0 if p_par == 0 else 3
                slot_order = [s for s in range(4) if s != shared_slot] + [shared_slot]
                for si, slot in enumerate(slot_order):
                    wi, half = slot // 2, slot % 2
                    wslot = wtk[wi][0]
                    tgt = pset[slot]
                    if si == 3:
                        if p_par == 0:
                            wait("t", "cV", hmul_hist[-1])
                        else:
                            wait("t", "cS", silu_last[0])
                    for ft in range(NT):
                        last_mm = pe(lambda h, wslot=wslot, half=half, ft=ft,
                                     tgt=tgt: h.matmul(
                            out=tgt, lhsT=wgT[:, wslot, ft, :],
                            rhs=big[:, ft, 512 * half:512 * (half + 1)],
                            start=(ft == 0), stop=(ft == NT - 1)))
                gu_t = inc(last_mm, "cT")
                for wslot, toks in wtk:
                    wg_used[wslot] = gu_t
                nxt = it_g + 1
                if nxt < N_IT:
                    for w in range(2):
                        step_idx = 2 * nxt + w
                        tok = gu_pre[0]
                        if step_idx + 1 < len(gu_steps):
                            if itl == DN_BLK - 1 and w == 1:
                                pend[0] = step_idx + 1
                            else:
                                gu_pre[0] = gu_load(step_idx + 1)
                        prepped.append(gu_prep(step_idx, tok))
                wait("s", "cT", gu_t)
                wait("s", "cV", hmul_hist[-1])
                si_s = 0
                for half in range(2):
                    tgt = pset[half]
                    isc = sc(lambda h, half=half, tgt=tgt: h.activation(
                        out=gbuf[:, 512 * half:512 * (half + 1)], in_=tgt,
                        func=AF.Silu))
                    si_s = inc(isc, "cS")
                silu_last[0] = si_s
                wait("v", "cT", gu_t)
                wait("v", "cS", si_s)
                drain("v")
                if blk > 0 and itl == 0:
                    wait("v", "cT", prev_dn_t)
                for half in range(2):
                    tgt = pset[2 + half]
                    iv = v(lambda h, itl=itl, half=half, tgt=tgt: h.tensor_tensor(
                        out=big[:, 2 * itl, 1024 + 512 * half:1024 + 512 * (half + 1)],
                        in0=gbuf[:, 512 * half:512 * (half + 1)],
                        in1=tgt, op=ALU.mult))
                    hmul_hist.append(inc(iv, "cV"))

            wait("t", "cV", hmul_hist[-1])
            add_hist = []
            it_i = 0
            deq_cv = []
            for seg in range(8):
                slot = seg % 2
                wait("y", "cV", fconsumed[slot])
                tok = dma("dL", lambda h, seg=seg, blk=blk, slot=slot: h.dma_start(
                    out=fslots[slot].rearrange("p (a c) -> p a c", c=256),
                    in_=dw_h.ap()[1024 * blk:1024 * (blk + 1),
                                  256 * seg:256 * (seg + 1)].rearrange(
                        "(a p) c -> p a c", p=128)))
                waitt("v", tok)
                if seg == 0:
                    wait("v", "cT", prev_dn_t)
                iv = v(lambda h, seg=seg, slot=slot: h.tensor_scalar(
                    out=wdT[:, :, 256 * seg:256 * (seg + 1)],
                    in0=fslots[slot].rearrange("p (a c) -> p a c", c=256),
                    scalar1=128.0, scalar2=None, op0=ALU.subtract))
                cv = inc(iv, "cV")
                fconsumed[slot] = cv
                deq_cv.append(cv)
            mul_cv = 0
            for ec in range(4):
                wait("t", "cV", deq_cv[2 * ec + 1])
                for tt in range(NOWN):
                    tgt = ps_b if it_i % 2 == 0 else ps_c
                    if len(add_hist) >= 2:
                        wait("t", "cV", add_hist[-2])
                    last_mm = NULL
                    for jt in range(DN_BLK):
                        last_mm = pe(lambda h, tt=tt, ec=ec, jt=jt, tgt=tgt: h.matmul(
                            out=tgt[:, :],
                            lhsT=big[:, 2 * jt, 1024 + 128 * tt:1024 + 128 * (tt + 1)],
                            rhs=wdT[:, jt, 512 * ec:512 * (ec + 1)],
                            start=(jt == 0), stop=(jt == DN_BLK - 1)))
                    dn_t = inc(last_mm, "cT")
                    wait("v", "cT", dn_t)
                    iv = v(lambda h, ec=ec, tgt=tgt: h.tensor_tensor(
                        out=tgt[:, :], in0=tgt[:, :],
                        in1=sexp[:, 512 * ec:512 * (ec + 1)], op=ALU.mult))
                    mul_cv = inc(iv, "cV")
                    wait("v", "cV", mul_cv)
                    iv = v(lambda h, tt=tt, ec=ec, tgt=tgt: h.tensor_tensor(
                        out=acc[:, tt, 512 * ec:512 * (ec + 1)],
                        in0=acc[:, tt, 512 * ec:512 * (ec + 1)],
                        in1=tgt[:, :], op=ALU.add))
                    add_hist.append(inc(iv, "cV"))
                    it_i += 1
            last_down_add[0] = add_hist[-1]
            prev_dn_t = cnt["cT"]
            if pend[0] is not None:
                gu_pre[0] = gu_load(pend[0])

        ffn_done = cnt["cV"]
        wait("y", "cV", ffn_done)
        for t in range(NOWN):
            st = sy(lambda h, t=t: h.dma_start(
                out=out_h.ap()[128 * t:128 * (t + 1), :], in_=acc[:, t, :]))
            inc(st, "dO", 16)
        wait("y", "dO", cnt["dO"])

    es_a = ExitStack()

    def mka(name, shape, dt):
        return es_a.enter_context(nc.sbuf_tensor(name, shape, dt))

    U = mka("U", [128, 2 * WMAX], BF16)
    TA = {
        "xnownT": mka("xnownT", [128, NT, 1024], BF16),
        "kT": mka("kT", [128, 2052], BF16),
        "vb": mka("vb", [128, 17, HD], BF16),
        "qTb": mka("qTb", [128, 1024], BF16),
        "aoT": mka("aoT", [128, 2, 1024], BF16),
        "U": U,
        "ext": mka("ext", [128, 2 * KTOP], F32),
        "mask": mka("mask", [128, 260], F32),
        "smask": mka("smask", [128, 2052], F32),
        "rs2": mka("rs2", [128, 1796], F32),
        "rs3": mka("rs3", [128, 1540], F32),
        "kqc": mka("kqc", [128, 1024], BF16),
    }

    snap_a = None
    with nc.Block("phase_a") as block:
        @block.sync
        def _(h):
            state["eng"], state["h"] = "y", h
            for k in cnt:
                cnt[k] = 0
            reset_rr()
            program_a(TA)

        @block.vector
        def _(h):
            state["eng"], state["h"] = "v", h
            for k in cnt:
                cnt[k] = 0
            reset_rr()
            program_a(TA)

        @block.scalar
        def _(h):
            state["eng"], state["h"] = "s", h
            for k in cnt:
                cnt[k] = 0
            reset_rr()
            program_a(TA)

        @block.tensor
        def _(h):
            state["eng"], state["h"] = "t", h
            for k in cnt:
                cnt[k] = 0
            reset_rr()
            program_a(TA)

        @block.gpsimd
        def _(h):
            state["eng"], state["h"] = "g", h
            for k in cnt:
                cnt[k] = 0
            reset_rr()
            program_a(TA)
    snap_a = dict(cnt)
    rr_a = dict(rr)
    barrier = (snap_a["cV"], snap_a["cS"], snap_a["cT"])

    es_a.close()

    es_b = ExitStack()

    def mkb(name, shape, dt):
        return es_b.enter_context(nc.sbuf_tensor(name, shape, dt))

    TB = {
        "fraw": mkb("fraw", [128, E], I32),
        "fbf": mkb("fbf", [128, E], BF16),
        "fbf2": mkb("fbf2", [128, E], BF16),
        "wgT": mkb("wgT", [128, 4, NT, 128], BF16),
        "wdT": mkb("wdT", [128, DN_BLK, E], BF16),
        "gbuf": mkb("gbuf", [128, 1024], BF16),
        "sexp": mkb("sexp", [128, E], F32),
        "onesB": mkb("onesB", [1, 128], F32),
    }

    with nc.Block("phase_b") as block:
        @block.sync
        def _(h):
            state["eng"], state["h"] = "y", h
            cnt.update(snap_a)
            rr.update(rr_a)
            program_b(TB, barrier)

        @block.vector
        def _(h):
            state["eng"], state["h"] = "v", h
            cnt.update(snap_a)
            rr.update(rr_a)
            program_b(TB, barrier)

        @block.scalar
        def _(h):
            state["eng"], state["h"] = "s", h
            cnt.update(snap_a)
            rr.update(rr_a)
            program_b(TB, barrier)

        @block.tensor
        def _(h):
            state["eng"], state["h"] = "t", h
            cnt.update(snap_a)
            rr.update(rr_a)
            program_b(TB, barrier)

        @block.gpsimd
        def _(h):
            state["eng"], state["h"] = "g", h
            cnt.update(snap_a)
            rr.update(rr_a)
            program_b(TB, barrier)

    es_b.close()
    es_glob.close()
    return nc


_NC_CACHE = None


def kernel(L0_attn_x, L0_attn_Wq, L0_attn_bq, L0_attn_Wk, L0_attn_bk,
           L0_attn_Wv, L0_attn_bv, L0_attn_Wo, L0_attn_bo, L0_attn_pk,
           L0_attn_pv, L0_ffn_gate_w, L0_ffn_gate_scale, L0_ffn_up_w,
           L0_ffn_up_scale, L0_ffn_down_w, L0_ffn_down_scale):
    global _NC_CACHE
    x = np.ascontiguousarray(np.asarray(L0_attn_x, dtype=np.float32))
    Wq = np.ascontiguousarray(np.asarray(L0_attn_Wq, dtype=np.float32))
    Wk = np.ascontiguousarray(np.asarray(L0_attn_Wk, dtype=np.float32))
    Wv = np.ascontiguousarray(np.asarray(L0_attn_Wv, dtype=np.float32))
    Wo = np.ascontiguousarray(np.asarray(L0_attn_Wo, dtype=np.float32))
    bq = np.asarray(L0_attn_bq, dtype=np.float32)
    bk = np.asarray(L0_attn_bk, dtype=np.float32)
    bv = np.asarray(L0_attn_bv, dtype=np.float32)
    bo = np.asarray(L0_attn_bo, dtype=np.float32)
    pk = np.asarray(L0_attn_pk, dtype=np.float32)[0]
    pv = np.asarray(L0_attn_pv, dtype=np.float32)[0]
    gw = np.ascontiguousarray(np.asarray(L0_ffn_gate_w, dtype=np.int32))
    uw = np.ascontiguousarray(np.asarray(L0_ffn_up_w, dtype=np.int32))
    dwT = np.ascontiguousarray(np.asarray(L0_ffn_down_w, dtype=np.int32).T)
    gs = np.asarray(L0_ffn_gate_scale, dtype=np.float32)
    us = np.asarray(L0_ffn_up_scale, dtype=np.float32)
    ds = np.asarray(L0_ffn_down_scale, dtype=np.float32)

    bqkT = np.ascontiguousarray(
        np.concatenate([bq.reshape(NT, 128).T, bk.reshape(NT, 128).T], axis=1))
    pkT = np.ascontiguousarray(pk.T.reshape(NT, 128, P).transpose(1, 0, 2).reshape(128, NT * P))
    gsT = np.ascontiguousarray(gs.reshape(N_IT, 128).T)
    usT = np.ascontiguousarray(us.reshape(N_IT, 128).T)

    if _NC_CACHE is None:
        _NC_CACHE = build_program()
    nc = _NC_CACHE

    in_maps = []
    for c in range(8):
        b, qh = c // 2, c % 2
        own_tiles = [2 * j + qh for j in range(NOWN)]
        xb = np.ascontiguousarray(x[b])
        xo = np.ascontiguousarray(
            np.concatenate([xb[128 * t:128 * (t + 1)] for t in own_tiles], axis=0))
        qoff = np.full((128, 1), 128.0 * qh, dtype=np.float32)
        in_maps.append({
            "xfull": xb, "xown": xo, "Wq": Wq, "Wk": Wk, "Wv": Wv, "Wo": Wo,
            "bqkT": bqkT, "bv_r": bv.reshape(1, E), "bo_r": bo.reshape(1, E),
            "pkT": pkT, "pv_r": np.ascontiguousarray(pv), "qoff": qoff,
            "gw": gw, "uw": uw, "dwT": dwT, "gsT": gsT, "usT": usT,
            "ds_r": ds.reshape(1, E),
        })

    res = run_bass_kernel_spmd(nc, in_maps, core_ids=list(range(8)))
    out = np.empty((B, S, E), dtype=np.float32)
    for c in range(8):
        b, qh = c // 2, c % 2
        o = np.asarray(res.results[c]["out"], dtype=np.float32)
        for j in range(NOWN):
            t = 2 * j + qh
            out[b, 128 * t:128 * (t + 1), :] = o[128 * j:128 * (j + 1), :]
    return out


if __name__ == "__main__":
    nc = build_program()
    print("built ok")


# version 21
# speedup vs baseline: 1.0264x; 1.0264x over previous
import math
import numpy as np
import concourse.bass as bass
import concourse.mybir as mybir
from concourse.bass_utils import run_bass_kernel_spmd
from contextlib import ExitStack

F32 = mybir.dt.float32
BF16 = mybir.dt.bfloat16
I32 = mybir.dt.int32
AF = mybir.ActivationFunctionType
ALU = mybir.AluOpType
AX = mybir.AxisListType

B, S, E = 4, 2048, 2048
HEADS, HD = 16, 128
P = 4
KTOP = 64
I_FFN = 8192
NT = 16
NOWN = 8
GH = 2
NG = HEADS // GH
SCALE = 1.0 / math.sqrt(HD)
NEG = -1.0e30
NEGT = -1.0e29
EPS = 1e-5
N_IT = I_FFN // 128
DN_BLK = 8
WMAX = 2176


def wj(j):
    return 256 * j + 384


def build_program():
    nc = bass.Bass("TRN2", target_bir_lowering=False, dynamic_dma_scratch_size=512)

    xfull = nc.dram_tensor("xfull", [S, E], F32, kind="ExternalInput")
    xown = nc.dram_tensor("xown", [NOWN * 128, E], F32, kind="ExternalInput")
    Wq_h = nc.dram_tensor("Wq", [E, E], F32, kind="ExternalInput")
    Wk_h = nc.dram_tensor("Wk", [E, E], F32, kind="ExternalInput")
    Wv_h = nc.dram_tensor("Wv", [E, E], F32, kind="ExternalInput")
    Wo_h = nc.dram_tensor("Wo", [E, E], F32, kind="ExternalInput")
    bqkT = nc.dram_tensor("bqkT", [128, 2 * NT], F32, kind="ExternalInput")
    bv_r = nc.dram_tensor("bv_r", [1, E], F32, kind="ExternalInput")
    bo_r = nc.dram_tensor("bo_r", [1, E], F32, kind="ExternalInput")
    pkT_h = nc.dram_tensor("pkT", [128, NT * P], F32, kind="ExternalInput")
    pv_h = nc.dram_tensor("pv_r", [P, E], F32, kind="ExternalInput")
    qoff_h = nc.dram_tensor("qoff", [128, 1], F32, kind="ExternalInput")
    gw_h = nc.dram_tensor("gw", [I_FFN, E], I32, kind="ExternalInput")
    uw_h = nc.dram_tensor("uw", [I_FFN, E], I32, kind="ExternalInput")
    dw_h = nc.dram_tensor("dwT", [I_FFN, E], I32, kind="ExternalInput")
    gsT_h = nc.dram_tensor("gsT", [128, N_IT], F32, kind="ExternalInput")
    usT_h = nc.dram_tensor("usT", [128, N_IT], F32, kind="ExternalInput")
    dsr_h = nc.dram_tensor("ds_r", [1, E], F32, kind="ExternalInput")
    out_h = nc.dram_tensor("out", [NOWN * 128, E], F32, kind="ExternalOutput")

    es_glob = ExitStack()

    def mkg(name, shape, dt):
        return es_glob.enter_context(nc.sbuf_tensor(name, shape, dt))

    acc = mkg("acc", [128, NOWN, E], F32)
    big = mkg("big", [128, NT, E], BF16)
    pool = mkg("pool", [128, E], F32)
    wcast = mkg("wcast", [128, E], BF16)
    wbf = wcast[:, 0:E]
    stats = mkg("stats", [128, 6, 24], F32)
    cols = mkg("cols", [128, 8], F32)
    biasq = mkg("biasq", [128, 2 * NT], F32)
    gscol = mkg("gscol", [128, N_IT], F32)
    uscol = mkg("uscol", [128, N_IT], F32)
    brow = mkg("brow", [1, 256], BF16)
    pkcol = mkg("pkcol", [128, NT, P], F32)
    qoff = mkg("qoff_sb", [128, 1], F32)

    ps_s = es_glob.enter_context(nc.psum_tensor("ps_s", [128, WMAX], F32))
    ps_a = es_glob.enter_context(nc.psum_tensor("ps_a", [128, 512], F32))
    ps_b = es_glob.enter_context(nc.psum_tensor("ps_b", [128, 512], F32))
    ps_c = es_glob.enter_context(nc.psum_tensor("ps_c", [128, 512], F32))

    sems = {}
    cnt = {}
    for name in ["dL0", "dL1", "dL2", "dL3", "dT0", "dT1", "dC", "dO",
                 "cV", "cS", "cT", "cG"]:
        sems[name] = es_glob.enter_context(nc.semaphore(name))
        cnt[name] = 0

    state = {"eng": None, "h": None}

    class _Null:
        def then_inc(self, *a, **k):
            return self
    NULL = _Null()

    def on(eng):
        return state["eng"] == eng

    def emit(eng, fn):
        if on(eng):
            return fn(state["h"])
        return NULL

    def inc(ins, name, by=1):
        cnt[name] += by
        ins.then_inc(sems[name], by)
        return cnt[name]

    def wait(eng, name, val):
        if on(eng) and val > 0:
            state["h"].wait_ge(sems[name], val)

    def drain(eng):
        if on(eng):
            state["h"].drain()

    rr = {"dL": 0, "dT": 0}
    POOLS = {"dL": ["dL0", "dL1", "dL2", "dL3"], "dT": ["dT0", "dT1"]}

    def dma(pool_name, fn, eng="y"):
        sname = POOLS[pool_name][rr[pool_name] % len(POOLS[pool_name])]
        rr[pool_name] += 1
        wait(eng, sname, cnt[sname])
        ins = emit(eng, fn)
        val = inc(ins, sname, 16)
        return (sname, val)

    def waitt(eng, tok):
        wait(eng, tok[0], tok[1])

    def reset_rr():
        rr["dL"] = 0
        rr["dT"] = 0

    def wait_pool(eng, pool_name):
        for sname in POOLS[pool_name]:
            wait(eng, sname, cnt[sname])

    v = lambda fn: emit("v", fn)
    sc = lambda fn: emit("s", fn)
    pe = lambda fn: emit("t", fn)
    gp = lambda fn: emit("g", fn)
    sy = lambda fn: emit("y", fn)

    def program_a(T):
        xnownT, kT, vb, qTb, aoT, U, ext, mask, smask, rs2, rs3, kqc = (
            T["xnownT"], T["kT"], T["vb"], T["qTb"], T["aoT"], T["U"],
            T["ext"], T["mask"], T["smask"], T["rs2"], T["rs3"], T["kqc"])
        ebuf = U[:, 0:WMAX]
        pT3 = lambda nblk: U[:, WMAX:WMAX + 128 * nblk].rearrange(
            "p (a b) -> p a b", b=128)
        pTs = lambda blk: U[:, WMAX + 128 * blk:WMAX + 128 * (blk + 1)]
        wob_sl = lambda hh, a, b: U[:, 2048 * hh + a:2048 * hh + b]

        inc(sy(lambda h: h.dma_start(out=biasq[:, :], in_=bqkT.ap())), "dC", 16)
        inc(sy(lambda h: h.dma_start(
            out=pkcol[:, :, :].rearrange("p a b -> p (a b)"),
            in_=pkT_h.ap())), "dC", 16)
        inc(sy(lambda h: h.dma_start(out=gscol[:, :], in_=gsT_h.ap())), "dC", 16)
        inc(sy(lambda h: h.dma_start(out=uscol[:, :], in_=usT_h.ap())), "dC", 16)
        inc(sy(lambda h: h.dma_start(out=qoff[:, :], in_=qoff_h.ap())), "dC", 16)
        small_ld = cnt["dC"]

        ig = gp(lambda h: h.iota(mask[:, :], pattern=[[1, 260]], base=0,
                                 channel_multiplier=-1,
                                 allow_small_or_imprecise_dtypes=True))
        g_iota = inc(ig, "cG")

        wait("v", "dC", small_ld)
        wait("v", "cG", g_iota)
        v(lambda h: h.tensor_scalar(out=mask[:, :], in0=mask[:, :], scalar1=qoff[:, :],
                                    scalar2=None, op0=ALU.subtract))
        drain("v")
        v(lambda h: h.tensor_scalar(out=mask[:, :], in0=mask[:, :], scalar1=4.0,
                                    scalar2=NEG, op0=ALU.is_gt, op1=ALU.mult))
        drain("v")
        v(lambda h: h.memset(brow[0:1, 0:128], 1.0))
        iv = v(lambda h: h.tensor_scalar(out=biasq[:, 0:NT], in0=biasq[:, 0:NT],
                                         scalar1=SCALE, scalar2=None, op0=ALU.mult))
        const_done = inc(iv, "cV")

        last_sq = 0
        for t in range(NT + NOWN):
            if t == 0:
                wait("y", "cV", const_done)
            if t < NT:
                tok = dma("dL", lambda h, t=t: h.dma_start(
                    out=pool[:, :], in_=xfull.ap()[128 * t:128 * (t + 1), :]))
            else:
                tok = dma("dL", lambda h, t=t: h.dma_start(
                    out=acc[:, t - NT, :],
                    in_=xown.ap()[128 * (t - NT):128 * (t - NT + 1), :]))
            waitt("v", tok)
            waitt("s", tok)
            srcap = (lambda t: pool[:, :] if t < NT else acc[:, t - NT, :])(t)
            iv = v(lambda h, t=t, srcap=srcap: h.reduce_sum(
                stats[:, 0, t:t + 1], srcap, axis=AX.X))
            rs_v = inc(iv, "cV")
            isq = sc(lambda h, t=t, srcap=srcap: h.activation(
                out=ps_s[:, 0:E], in_=srcap, func=AF.Square,
                accum_out=stats[:, 1, t:t + 1]))
            last_sq = inc(isq, "cS")
            drain("s")
            if t < NT - 1:
                wait("y", "cS", last_sq)
                wait("y", "cV", rs_v)
        wait("v", "cS", last_sq)
        drain("v")
        v(lambda h: h.tensor_scalar(out=stats[:, 2, :], in0=stats[:, 0, :],
                                    scalar1=1.0 / E, scalar2=None, op0=ALU.mult))
        v(lambda h: h.tensor_scalar(out=stats[:, 3, :], in0=stats[:, 1, :],
                                    scalar1=1.0 / E, scalar2=None, op0=ALU.mult))
        drain("v")
        v(lambda h: h.tensor_tensor(out=stats[:, 4, :], in0=stats[:, 2, :],
                                    in1=stats[:, 2, :], op=ALU.mult))
        drain("v")
        v(lambda h: h.tensor_tensor(out=stats[:, 3, :], in0=stats[:, 3, :],
                                    in1=stats[:, 4, :], op=ALU.subtract))
        drain("v")
        iv = v(lambda h: h.tensor_scalar(out=stats[:, 3, :], in0=stats[:, 3, :],
                                         scalar1=EPS, scalar2=None, op0=ALU.add))
        st_v = inc(iv, "cV")
        wait("s", "cV", st_v)
        isq = sc(lambda h: h.activation(out=stats[:, 4, :], in_=stats[:, 3, :],
                                        func=AF.Sqrt))
        st_s = inc(isq, "cS")
        wait("v", "cS", st_s)
        iv = v(lambda h: h.reciprocal(stats[:, 5, :], stats[:, 4, :]))
        drain("v")
        stats_done = inc(iv, "cV")

        prev_tr = None
        wait("y", "cV", stats_done)
        wait("y", "cS", st_s)
        for t in range(NT + NOWN):
            if t < NT:
                tok = dma("dL", lambda h, t=t: h.dma_start(
                    out=pool[:, :], in_=xfull.ap()[128 * t:128 * (t + 1), :]))
                waitt("v", tok)
            wait("v", "cV", stats_done)
            if prev_tr is not None:
                waitt("v", prev_tr)
            srcap = (lambda t: pool[:, :] if t < NT else acc[:, t - NT, :])(t)
            iv = v(lambda h, t=t, srcap=srcap: h.scalar_tensor_tensor(
                out=wbf, in0=srcap, scalar=stats[:, 2, t:t + 1],
                in1=stats[:, 5, t:t + 1].to_broadcast([128, E]),
                op0=ALU.subtract, op1=ALU.mult))
            nv = inc(iv, "cV")
            wait("y", "cV", nv)
            if t < NT:
                prev_tr = dma("dT", lambda h, t=t: h.dma_start_transpose(
                    out=big[:, :, 128 * t:128 * (t + 1)], in_=wbf))
            else:
                prev_tr = dma("dT", lambda h, t=t: h.dma_start_transpose(
                    out=xnownT[:, :, 128 * (t - NT):128 * (t - NT + 1)], in_=wbf))

        kq_used = [0, 0]
        xnb_first = [prev_tr]
        att_state = {"expO_prev": 0, "next": None, "av_prev": 0, "psa": 0,
                     "sel_prev": 0, "mv_prev": 0,
                     "pool_cv": cnt["cV"], "pool_cs": 0}
        es_tok = [None, None, None, None]
        wo_mm_hist = [0]
        for g in range(HEADS):
            ct = g
            c0 = 128 * g

            wait("y", "cV", att_state["pool_cv"])
            wait("y", "cS", att_state["pool_cs"])
            for _sl in (0, 1):
                if es_tok[_sl] is not None:
                    for _t in es_tok[_sl]:
                        waitt("y", _t)
            tok = dma("dL", lambda h, ct=ct: h.dma_start(
                out=pool[:, :].rearrange("p (a c) -> p a c", c=128),
                in_=Wk_h.ap()[:, 128 * ct:128 * (ct + 1)].rearrange(
                    "(a p) c -> p a c", p=128)))
            last_mm = NULL
            cv = 0
            for c in range(4):
                if c == 0:
                    waitt("v", tok)
                wait("v", "cT", kq_used[c % 2])
                iv = v(lambda h, c=c: h.tensor_copy(
                    kqc[:, 512 * (c % 2):512 * (c % 2) + 512],
                    pool[:, 512 * c:512 * (c + 1)]))
                cv = inc(iv, "cV")
                wait("t", "cV", cv)
                if g == 0 and c == 0:
                    wait_pool("t", "dT")
                if c == 0:
                    wait("t", "cS", cnt["cS"])
                for fi in range(4):
                    for ch in range(4):
                        last_mm = pe(lambda h, c=c, fi=fi, ch=ch: h.matmul(
                            out=ps_s[:, 512 * ch:512 * (ch + 1)],
                            lhsT=kqc[:, 512 * (c % 2) + 128 * fi:
                                     512 * (c % 2) + 128 * (fi + 1)],
                            rhs=big[:, 4 * c + fi, 512 * ch:512 * (ch + 1)],
                            start=(c == 0 and fi == 0),
                            stop=(c == 3 and fi == 3)))
                kq_used[c % 2] = inc(last_mm, "cT")
            prev_pe = kq_used[1]
            wait("s", "cT", prev_pe)
            isc = sc(lambda h, ct=ct: h.activation(
                out=kT[:, 0:2048], in_=ps_s[:, 0:2048], func=AF.Identity,
                bias=biasq[:, NT + ct:NT + ct + 1], scale=1.0))
            sv = inc(isc, "cS")
            wait("t", "cS", sv)
            wait("g", "cS", sv)
            ig = gp(lambda h, ct=ct: h.tensor_copy(
                kT[:, 2048:2048 + P], pkcol[:, ct, :]))
            kT_g = inc(ig, "cG")

            wait("y", "cV", cv)
            tok = dma("dL", lambda h, ct=ct: h.dma_start(
                out=pool[:, :].rearrange("p (a c) -> p a c", c=128),
                in_=Wq_h.ap()[:, 128 * ct:128 * (ct + 1)].rearrange(
                    "(a p) c -> p a c", p=128)))
            last_mm = NULL
            for c in range(4):
                if c == 0:
                    waitt("v", tok)
                wait("v", "cT", kq_used[c % 2])
                iv = v(lambda h, c=c: h.tensor_copy(
                    kqc[:, 512 * (c % 2):512 * (c % 2) + 512],
                    pool[:, 512 * c:512 * (c + 1)]))
                cv = inc(iv, "cV")
                wait("t", "cV", cv)
                if c == 0:
                    wait("t", "cS", cnt["cS"])
                for fi in range(4):
                    for half in range(2):
                        tgt = ps_a if half == 0 else ps_b
                        last_mm = pe(lambda h, c=c, fi=fi, half=half, tgt=tgt:
                                     h.matmul(
                            out=tgt[:, :],
                            lhsT=kqc[:, 512 * (c % 2) + 128 * fi:
                                     512 * (c % 2) + 128 * (fi + 1)],
                            rhs=xnownT[:, 4 * c + fi, 512 * half:512 * (half + 1)],
                            start=(c == 0 and fi == 0),
                            stop=(c == 3 and fi == 3)))
                kq_used[c % 2] = inc(last_mm, "cT")
            prev_pe = kq_used[1]
            for half in range(2):
                tgt = ps_a if half == 0 else ps_b
                wait("s", "cT", prev_pe)
                isc = sc(lambda h, ct=ct, half=half, tgt=tgt: h.activation(
                    out=qTb[:, 512 * half:512 * (half + 1)], in_=tgt[:, :],
                    func=AF.Identity, bias=biasq[:, ct:ct + 1], scale=SCALE))
                sv = inc(isc, "cS")
                wait("t", "cS", sv)

            wait("y", "cV", cv)
            tok = dma("dL", lambda h, c0=c0: h.dma_start(
                out=pool[:, :].rearrange("p (a c) -> p a c", c=128),
                in_=Wv_h.ap()[:, c0:c0 + 128].rearrange("(a p) c -> p a c", p=128)))
            waitt("v", tok)
            wait("v", "cT", cnt["cT"])
            iv = v(lambda h: h.tensor_copy(U[:, 0:2048], pool[:, :]))
            wv_done = inc(iv, "cV")
            wait("y", "cV", wv_done)
            tok = dma("dL", lambda h, c0=c0: h.dma_start(
                out=pool[64:64 + P, 0:128], in_=pv_h.ap()[:, c0:c0 + 128]))
            waitt("g", tok)
            wait("g", "cT", cnt["cT"])
            gp(lambda h: h.memset(vb[:, 16, :], 0.0))
            drain("g")
            ig = gp(lambda h: h.tensor_copy(vb[0:P, 16, :], pool[64:64 + P, 0:128]))
            pv_g = inc(ig, "cG")
            tokb = dma("dL", lambda h, c0=c0: h.dma_start(
                out=pool[96:97, 0:128], in_=bv_r.ap()[0:1, c0:c0 + 128]))
            waitt("v", tokb)
            wait("v", "cT", cnt["cT"])
            iv = v(lambda h: h.tensor_copy(brow[0:1, 128:256], pool[96:97, 0:128]))
            bv_cv = inc(iv, "cV")

            ev_hist = []
            for tt in range(NT):
                if tt == 0:
                    wait("t", "cV", bv_cv)
                vtgt = ps_c if tt % 2 == 0 else ps_b
                if len(ev_hist) >= 2:
                    wait("t", "cV", ev_hist[-2])
                last_mm = NULL
                for ft in range(NT):
                    last_mm = pe(lambda h, tt=tt, ft=ft, vtgt=vtgt: h.matmul(
                        out=vtgt[:, 0:128], lhsT=big[:, ft, 128 * tt:128 * (tt + 1)],
                        rhs=U[:, 128 * ft:128 * (ft + 1)],
                        start=(ft == 0), stop=False))
                last_mm = pe(lambda h, vtgt=vtgt: h.matmul(
                    out=vtgt[:, 0:128], lhsT=brow[0:1, 0:128],
                    rhs=brow[0:1, 128:256], start=False, stop=True))
                tv = inc(last_mm, "cT")
                wait("v", "cT", tv)
                iv = v(lambda h, tt=tt, vtgt=vtgt: h.tensor_copy(
                    vb[:, tt, :], vtgt[:, 0:128]))
                ev_hist.append(inc(iv, "cV"))
            v_done = cnt["cV"]

            poolb = pool[:, :].bitcast(BF16)
            RSArr = [rs2[:, 0:1796], smask[:, 0:2052], rs3[:, 0:1540]]
            ES_map = [0, 1, 2, 3, 0, 1, 2, 3]
            ES = [poolb[:, 0:1536], poolb[:, 1536:3200],
                  wcast[:, 0:1920], ebuf[:, 0:2176]]

            def tW(t):
                return wj(t) - 124

            def RS(t):
                return RSArr[t % 3]

            def emit_scores(t, gate_cs, gate_cv):
                W = tW(t)
                wait("t", "cS", gate_cs)
                wait("t", "cV", max(v_done, gate_cv))
                wait("t", "cG", kT_g)
                off = 0
                lm = NULL
                while off < W:
                    n = min(512, W - off)
                    lm = pe(lambda h, t=t, off=off, n=n: h.matmul(
                        out=ps_s[:, off:off + n],
                        lhsT=qTb[:, 128 * t:128 * (t + 1)],
                        rhs=kT[:, off:off + n], start=True, stop=True))
                    off += n
                return inc(lm, "cT")

            def prep_ops(t, sct, par):
                W = tW(t)
                nb = wj(t) // 128
                ec = 0 if par == 0 else 64
                cc = 0 if par == 0 else 4
                ops = []
                ops.append((lambda h, W=W: h.tensor_tensor(
                    out=ps_s[:, W - 260:W], in0=ps_s[:, W - 260:W],
                    in1=mask[:, 0:260], op=ALU.add),
                    [("cT", sct)], "mask"))
                ops.append((lambda h, W=W, ec=ec: h.max(
                    out=ext[:, ec:ec + 8], in_=ps_s[:, 0:W]), None, "max8"))
                ops.append((lambda h, ec=ec, cc=cc: h.tensor_scalar(
                    out=cols[:, cc:cc + 1], in0=ext[:, ec:ec + 1], scalar1=-1.0,
                    scalar2=None, op0=ALU.mult), None, None))
                es = ES[ES_map[t]]
                mg = list(es_tok[ES_map[t]] or [])
                ops.append((lambda h, W=W, nb=nb, es=es: h.memset(
                    es[:, W:128 * nb], 0.0), mg or None, None))
                return ops

            def emit_copy(t, max8_cv, rs_free_cv):
                W = tW(t)
                rs = RS(t)
                wait("s", "cV", max(max8_cv, rs_free_cv))
                isc = sc(lambda h, W=W, rs=rs: h.activation(
                    out=rs[:, 0:W], in_=ps_s[:, 0:W], func=AF.Identity))
                return inc(isc, "cS")

            def emit_exp(t, par, mv_val):
                W = tW(t)
                cc = 0 if par == 0 else 4
                es = ES[ES_map[t]]
                tok3 = es_tok[ES_map[t]]
                if tok3 is not None:
                    for _t in tok3:
                        waitt("s", _t)
                if ES_map[t] == 2 and xnb_first[0] is not None:
                    waitt("s", xnb_first[0])
                    xnb_first[0] = None
                if ES_map[t] in (0, 1):
                    wait("s", "cG", pv_g)
                wait("s", "cV", mv_val)
                isc = sc(lambda h, W=W, cc=cc, es=es: h.activation(
                    out=es[:, 0:W], in_=ps_s[:, 0:W], func=AF.Exp,
                    bias=cols[:, cc:cc + 1], scale=1.0))
                return inc(isc, "cS")

            def emit_prep_inline(t, sct, par, rs_free_cv):
                ops = prep_ops(t, sct, par)
                vals = []
                for fn, gates, _tag in ops:
                    for gate in (gates or []):
                        wait("v", gate[0], gate[1])
                    last = v(fn)
                    drain("v")
                    vals.append(inc(last, "cV"))
                cp = emit_copy(t, vals[1], rs_free_cv)
                return vals[-1], cp

            stA = att_state
            for p in range(NOWN // 2):
                jE, jO = 2 * p, 2 * p + 1
                WE, WO = tW(jE), tW(jO)
                nbE, nbO = wj(jE) // 128, wj(jO) // 128
                if p == 0:
                    scE = emit_scores(jE, stA["expO_prev"], stA["mv_prev"])
                    mvE, cpE = emit_prep_inline(jE, scE, 0, stA["sel_prev"])
                    evE = emit_exp(jE, 0, mvE)
                    scO = emit_scores(jO, evE, mvE)
                    mvO, cpO = emit_prep_inline(jO, scO, 1, stA["sel_prev"])
                    evO = emit_exp(jO, 1, mvO)
                else:
                    mvE, cpE, evE, mvO, cpO, evO = stA["next"]
                if p + 1 < NOWN // 2:
                    jE2, jO2 = 2 * p + 2, 2 * p + 3
                    if stA.get("scE_pre") and stA["scE_pre"][0] == jE2:
                        scE2 = stA["scE_pre"][1]
                    else:
                        scE2 = emit_scores(jE2, evO, mvO)
                    g1 = prep_ops(jE2, scE2, 0)
                else:
                    jE2 = jO2 = None
                    g1 = []
                wait("v", "cS", cpE)
                wait("v", "cS", cpO)
                lastE, lastO = 0, 0
                gq = list(g1)
                mvE2 = [None]
                cpE2 = [None]
                mkO2 = [None]
                for r in range(8):
                    if r > 0:
                        wait("v", "cV", lastE)
                        iv = v(lambda h, r=r, WE=WE, jE=jE: h.max(
                            out=ext[:, 8 * r:8 * (r + 1)], in_=RS(jE)[:, 0:WE]))
                        lastE = inc(iv, "cV")
                        wait("v", "cV", lastO)
                        iv = v(lambda h, r=r, WO=WO, jO=jO: h.max(
                            out=ext[:, 64 + 8 * r:64 + 8 * (r + 1)],
                            in_=RS(jO)[:, 0:WO]))
                        lastO = inc(iv, "cV")
                    else:
                        wait("v", "cV", mvE)
                        wait("v", "cV", mvO)
                    wait("v", "cV", lastE)
                    iv = v(lambda h, r=r, WE=WE, jE=jE: h.match_replace(
                        out=RS(jE)[:, 0:WE], in_to_replace=ext[:, 8 * r:8 * (r + 1)],
                        in_values=RS(jE)[:, 0:WE], imm_value=NEG))
                    lastE = inc(iv, "cV")
                    wait("v", "cV", lastO)
                    iv = v(lambda h, r=r, WO=WO, jO=jO: h.match_replace(
                        out=RS(jO)[:, 0:WO],
                        in_to_replace=ext[:, 64 + 8 * r:64 + 8 * (r + 1)],
                        in_values=RS(jO)[:, 0:WO], imm_value=NEG))
                    lastO = inc(iv, "cV")
                    if gq:
                        fn, gates, tag = gq.pop(0)
                        for gate in (gates or []):
                            wait("v", gate[0], gate[1])
                        wait("v", "cV", lastO)
                        gi = v(fn)
                        drain("v")
                        gv = inc(gi, "cV")
                        if tag == "max8":
                            if mkO2[0] is None and cpE2[0] is not None:
                                mkO2[0] = gv
                            elif cpE2[0] is None:
                                cpE2[0] = emit_copy(jE2, gv, stA["sel_prev"])
                        if not gq and mvE2[0] is None and jE2 is not None:
                            mvE2[0] = gv
                    if r == 5 and jE2 is not None:
                        evE2 = emit_exp(jE2, 0, mvE2[0])
                        scO2 = emit_scores(jO2, evE2, mvE2[0])
                        gq.extend(prep_ops(jO2, scO2, 1))
                wait("v", "cV", lastE)
                wait("v", "cS", evE)
                iv = v(lambda h, WE=WE, jE=jE: h.scalar_tensor_tensor(
                    out=ES[ES_map[jE]][:, 0:WE], in0=RS(jE)[:, 0:WE], scalar=NEGT,
                    in1=ES[ES_map[jE]][:, 0:WE], op0=ALU.is_lt, op1=ALU.mult,
                    accum_out=cols[:, 1:2]))
                pvE = inc(iv, "cV")
                drain("v")
                mvO2 = None
                for fn, gates, tag in gq:
                    for gate in (gates or []):
                        wait("v", gate[0], gate[1])
                    gi = v(fn)
                    drain("v")
                    mvO2 = inc(gi, "cV")
                    if tag == "max8" and mkO2[0] is None:
                        mkO2[0] = mvO2
                if jE2 is not None:
                    cpO2 = emit_copy(jO2, mkO2[0], pvE)
                    evO2 = emit_exp(jO2, 1, mvO2)
                    if jO2 + 1 <= 2 * (NOWN // 2 - 1):
                        stA["scE_pre"] = (jO2 + 1,
                                          emit_scores(jO2 + 1, evO2, mvO2))
                    else:
                        stA["scE_pre"] = None
                    stA["next"] = (mvE2[0], cpE2[0], evE2, mvO2, cpO2, evO2)
                    stA["expO_prev"] = evO2
                    stA["mv_prev"] = mvO2
                else:
                    stA["mv_prev"] = mvO
                wait("v", "cV", lastO)
                wait("v", "cS", evO)
                iv = v(lambda h, WO=WO, jO=jO: h.scalar_tensor_tensor(
                    out=ES[ES_map[jO]][:, 0:WO], in0=RS(jO)[:, 0:WO], scalar=NEGT,
                    in1=ES[ES_map[jO]][:, 0:WO], op0=ALU.is_lt, op1=ALU.mult,
                    accum_out=cols[:, 5:6]))
                pvO = inc(iv, "cV")
                stA["sel_prev"] = pvO
                drain("v")
                wait("v", "cS", stA["psa"])
                v(lambda h: h.reciprocal(cols[:, 2:3], cols[:, 1:2]))
                iv = v(lambda h: h.reciprocal(cols[:, 6:7], cols[:, 5:6]))
                rcp = inc(iv, "cV")
                for (jT, Wt, nbt, par) in ((jE, WE, nbE, 0), (jO, WO, nbO, 1)):
                    es = ES[ES_map[jT]]
                    cc = 2 if par == 0 else 6
                    wait("s", "cV", rcp)
                    isc = sc(lambda h, Wt=Wt, es=es, cc=cc: h.activation(
                        out=es[:, 0:Wt], in_=es[:, 0:Wt], func=AF.Identity,
                        scale=cols[:, cc:cc + 1]))
                    esc = inc(isc, "cS")
                    wait("y", "cS", esc)
                    wait("y", "cT", stA["av_prev"])
                    nb0 = (nbt + 1) // 2
                    t1 = dma("dT", lambda h, nb0=nb0, es=es: h.dma_start_transpose(
                        out=pT3(nb0), in_=es[:, 0:128 * nb0]))
                    t2 = dma("dT", lambda h, nb0=nb0, nbt=nbt, es=es:
                             h.dma_start_transpose(
                        out=U[:, WMAX + 128 * nb0:WMAX + 128 * nbt].rearrange(
                            "p (a b) -> p a b", b=128),
                        in_=es[:, 128 * nb0:128 * nbt]))
                    es_tok[ES_map[jT]] = [t1, t2]
                    waitt("t", t1)
                    waitt("t", t2)
                    wait("t", "cS", stA["psa"])
                    wait("t", "cG", pv_g)
                    nfull = min(nbt, 16)
                    lm = NULL
                    for blk in range(nfull):
                        lm = pe(lambda h, blk=blk, nfull=nfull, nbt=nbt: h.matmul(
                            out=ps_a[:, 0:128], lhsT=vb[:, blk, :],
                            rhs=pTs(blk), start=(blk == 0),
                            stop=(blk == nfull - 1 and nbt == nfull)))
                    if nbt > 16:
                        lm = pe(lambda h: h.matmul(
                            out=ps_a[:, 0:128], lhsT=vb[0:P, 16, :],
                            rhs=pTs(16)[0:P, :], start=False, stop=True))
                    av = inc(lm, "cT")
                    stA["av_prev"] = av
                    wait("s", "cT", av)
                    isc = sc(lambda h, g=g, jT=jT: h.activation(
                        out=aoT[:, g % 2, 128 * jT:128 * (jT + 1)],
                        in_=ps_a[:, 0:128], func=AF.Identity))
                    stA["psa"] = inc(isc, "cS")
                if p in (0, 2):
                    stA["pool_cv"] = pvO
                    stA["pool_cs"] = stA["psa"]

            if g % 2 == 1:
                wo_cp = 0
                for hh in range(2):
                    ct2 = g - 1 + hh
                    if hh == 0:
                        wait("y", "cV", att_state["pool_cv"])
                        wait("y", "cS", att_state["pool_cs"])
                    else:
                        wait("y", "cV", wo_cp)
                    for _sl in (0, 1):
                        if es_tok[_sl] is not None:
                            for _t in es_tok[_sl]:
                                waitt("y", _t)
                    tok = dma("dL", lambda h, ct2=ct2: h.dma_start(
                        out=pool[:, :], in_=Wo_h.ap()[128 * ct2:128 * (ct2 + 1), :]))
                    waitt("v", tok)
                    wait("v", "cT", cnt["cT"])
                    iv = v(lambda h, hh=hh: h.tensor_copy(
                        wob_sl(hh, 0, 2048), pool[:, :]))
                    wo_cp = inc(iv, "cV")
                if g == 1:
                    wait("y", "cV", wo_cp)
                    tokb = dma("dL", lambda h: h.dma_start(
                        out=pool[96:97, :], in_=bo_r.ap()))
                    waitt("v", tokb)
                    wait("v", "cV", att_state["sel_prev"])
                    iv = v(lambda h: h.tensor_copy(
                        smask[:, :].bitcast(BF16)[0:1, 0:2048], pool[96:97, :]))
                    wo_cp = inc(iv, "cV")
                wo_ld = cnt["cV"]
                wait("t", "cV", wo_ld)
                wait("t", "cS", cnt["cS"])
                add_hist = []
                for it_i, (tt, ec) in enumerate([(a, b) for a in range(NOWN)
                                                 for b in range(4)]):
                    tgt = ps_b if it_i % 2 == 0 else ps_c
                    if len(add_hist) >= 2:
                        wait("t", "cV", add_hist[-2])
                    last_mm = NULL
                    for hh in range(2):
                        last_mm = pe(lambda h, hh=hh, tt=tt, ec=ec, tgt=tgt: h.matmul(
                            out=tgt[:, :], lhsT=aoT[:, hh, 128 * tt:128 * (tt + 1)],
                            rhs=wob_sl(hh, 512 * ec, 512 * (ec + 1)),
                            start=(hh == 0), stop=(hh == 1 and g != 1)))
                    if g == 1:
                        last_mm = pe(lambda h, ec=ec, tgt=tgt: h.matmul(
                            out=tgt[:, :], lhsT=brow[0:1, 0:128],
                            rhs=smask[:, :].bitcast(BF16)[
                                0:1, 512 * ec:512 * (ec + 1)],
                            start=False, stop=True))
                    wt = inc(last_mm, "cT")
                    wait("v", "cT", wt)
                    iv = v(lambda h, tt=tt, ec=ec, tgt=tgt: h.tensor_tensor(
                        out=acc[:, tt, 512 * ec:512 * (ec + 1)],
                        in0=acc[:, tt, 512 * ec:512 * (ec + 1)],
                        in1=tgt[:, :], op=ALU.add))
                    add_hist.append(inc(iv, "cV"))
                wo_mm_hist[0] = cnt["cT"]
                att_state["pool_cv"] = max(att_state["pool_cv"], wo_cp)

    def program_b(T, barrier):
        fraw, fbf, fbf2, wgT, wdT, gbuf, sexp, onesB = (
            T["fraw"], T["fbf"], T["fbf2"], T["wgT"], T["wdT"], T["gbuf"],
            T["sexp"], T["onesB"])
        bar_v, bar_s, bar_t = barrier

        for e in ("y", "v", "s", "t", "g"):
            wait(e, "cV", bar_v)
            wait(e, "cS", bar_s)
            wait(e, "cT", bar_t)

        last_sq = 0
        for t in range(NOWN):
            v(lambda h, t=t: h.reduce_sum(stats[:, 0, t:t + 1], acc[:, t, :], axis=AX.X))
            isq = sc(lambda h, t=t: h.activation(
                out=ps_s[:, 0:E], in_=acc[:, t, :], func=AF.Square,
                accum_out=stats[:, 1, t:t + 1]))
            last_sq = inc(isq, "cS")
            drain("s")
        wait("v", "cS", last_sq)
        drain("v")
        v(lambda h: h.tensor_scalar(out=stats[:, 2, 0:NOWN], in0=stats[:, 0, 0:NOWN],
                                    scalar1=1.0 / E, scalar2=None, op0=ALU.mult))
        v(lambda h: h.tensor_scalar(out=stats[:, 3, 0:NOWN], in0=stats[:, 1, 0:NOWN],
                                    scalar1=1.0 / E, scalar2=None, op0=ALU.mult))
        drain("v")
        v(lambda h: h.tensor_tensor(out=stats[:, 4, 0:NOWN], in0=stats[:, 2, 0:NOWN],
                                    in1=stats[:, 2, 0:NOWN], op=ALU.mult))
        drain("v")
        v(lambda h: h.tensor_tensor(out=stats[:, 3, 0:NOWN], in0=stats[:, 3, 0:NOWN],
                                    in1=stats[:, 4, 0:NOWN], op=ALU.subtract))
        drain("v")
        iv = v(lambda h: h.tensor_scalar(out=stats[:, 3, 0:NOWN], in0=stats[:, 3, 0:NOWN],
                                         scalar1=EPS, scalar2=None, op0=ALU.add))
        st_v = inc(iv, "cV")
        wait("s", "cV", st_v)
        isq = sc(lambda h: h.activation(out=stats[:, 4, 0:NOWN], in_=stats[:, 3, 0:NOWN],
                                        func=AF.Sqrt))
        st_s = inc(isq, "cS")
        wait("v", "cS", st_s)
        iv = v(lambda h: h.reciprocal(stats[:, 5, 0:NOWN], stats[:, 4, 0:NOWN]))
        drain("v")
        stats2_done = inc(iv, "cV")

        prev_tr = None
        for t in range(NOWN):
            wait("v", "cV", stats2_done)
            if prev_tr is not None:
                waitt("v", prev_tr)
            iv = v(lambda h, t=t: h.scalar_tensor_tensor(
                out=wbf, in0=acc[:, t, :], scalar=stats[:, 2, t:t + 1],
                in1=stats[:, 5, t:t + 1].to_broadcast([128, E]),
                op0=ALU.subtract, op1=ALU.mult))
            nv = inc(iv, "cV")
            wait("y", "cV", nv)
            prev_tr = dma("dT", lambda h, t=t: h.dma_start_transpose(
                out=big[:, :, 128 * t:128 * (t + 1)], in_=wbf))

        tok = dma("dL", lambda h: h.dma_start(out=pool[0:1, :], in_=dsr_h.ap()))
        iv = v(lambda h: h.memset(onesB[0:1, :], 1.0))
        ones_cv = inc(iv, "cV")
        waitt("t", tok)
        wait("t", "cV", ones_cv)
        sx_t = 0
        sx_cv = 0
        for ec in range(4):
            wait("t", "cV", sx_cv)
            lm = pe(lambda h, ec=ec: h.matmul(
                out=ps_b[:, :], lhsT=onesB[0:1, :],
                rhs=pool[0:1, 512 * ec:512 * (ec + 1)], start=True, stop=True))
            sx_t = inc(lm, "cT")
            wait("v", "cT", sx_t)
            iv = v(lambda h, ec=ec: h.tensor_copy(
                sexp[:, 512 * ec:512 * (ec + 1)], ps_b[:, :]))
            sx_cv = inc(iv, "cV")
        prev_dn_t = 0
        fslots = [fraw[:, :], pool[:, :].bitcast(I32)]
        fconsumed = [0, sx_cv]
        bslots = [fbf[:, :], fbf2[:, :]]
        bguard = [None, None]
        wg_used = [0, 0, 0, 0]
        wg_tok = [None, None, None, None]
        hmul_hist = [0, 0]
        last_down_add = [0]
        silu_last = [0]
        gu_steps = []
        for blk in range(N_IT // DN_BLK):
            for itl in range(DN_BLK):
                it_g = blk * DN_BLK + itl
                gu_steps.append((it_g, 0, gw_h))
                gu_steps.append((it_g, 1, uw_h))
        gu_pre = [None]

        def gu_load(step_idx):
            it_g, which, w_h = gu_steps[step_idx]
            slot = step_idx % 2
            wait("y", "cV", fconsumed[slot])
            return dma("dL", lambda h, it_g=it_g, w_h=w_h, slot=slot: h.dma_start(
                out=fslots[slot], in_=w_h.ap()[128 * it_g:128 * (it_g + 1), :]))

        def gu_prep(step_idx, tok):
            it_g, which, w_h = gu_steps[step_idx]
            scol = gscol if which == 0 else uscol
            slot = step_idx % 2
            bslot = step_idx % 2
            wslot = 2 * (it_g % 2) + which
            waitt("v", tok)
            if bguard[bslot] is not None:
                for _g in bguard[bslot]:
                    waitt("v", _g)
            iv = v(lambda h, it_g=it_g, scol=scol, slot=slot, bslot=bslot:
                   h.scalar_tensor_tensor(
                out=bslots[bslot], in0=fslots[slot], scalar=128.0,
                in1=scol[:, it_g:it_g + 1].to_broadcast([128, E]),
                op0=ALU.subtract, op1=ALU.mult))
            cv = inc(iv, "cV")
            fconsumed[slot] = cv
            wait("s", "cV", cv)
            wait("s", "cT", wg_used[wslot])
            toks = []
            for hf in range(2):
                toks.append(dma("dT", lambda h, wslot=wslot, bslot=bslot, hf=hf:
                                h.dma_start_transpose(
                    out=wgT[:, wslot, 8 * hf:8 * (hf + 1), :],
                    in_=bslots[bslot][:, 1024 * hf:1024 * (hf + 1)]), eng="s"))
            bguard[bslot] = list(toks)
            return (wslot, toks)

        gu_pre[0] = gu_load(0)
        prepped = []
        for w in range(2):
            tok = gu_pre[0]
            gu_pre[0] = gu_load(w + 1)
            prepped.append(gu_prep(w, tok))
        for blk in range(N_IT // DN_BLK):
            pend = [None]
            for itl in range(DN_BLK):
                it_g = blk * DN_BLK + itl
                wtk = prepped
                prepped = []
                for wslot, toks in wtk:
                    for tk in toks:
                        waitt("t", tk)
                if blk == 0 and itl == 0:
                    wait_pool("t", "dT")
                psets = ([ps_a[:, 0:512], ps_b[:, 0:512],
                          ps_c[:, 0:512], ps_s[:, 0:512]],
                         [ps_s[:, 512:1024], ps_s[:, 1024:1536],
                          ps_s[:, 1536:2048], ps_a[:, 0:512]])
                p_par = it_g % 2
                pset = psets[p_par]
                wait("t", "cV", max(hmul_hist[-3] if len(hmul_hist) >= 3 else 0,
                                    last_down_add[0]))
                last_mm = NULL
                shared_slot = 0 if p_par == 0 else 3
                slot_order = [s for s in range(4) if s != shared_slot] + [shared_slot]
                for si, slot in enumerate(slot_order):
                    wi, half = slot // 2, slot % 2
                    wslot = wtk[wi][0]
                    tgt = pset[slot]
                    if si == 3:
                        if p_par == 0:
                            wait("t", "cV", hmul_hist[-1])
                        else:
                            wait("t", "cS", silu_last[0])
                    for ft in range(NT):
                        last_mm = pe(lambda h, wslot=wslot, half=half, ft=ft,
                                     tgt=tgt: h.matmul(
                            out=tgt, lhsT=wgT[:, wslot, ft, :],
                            rhs=big[:, ft, 512 * half:512 * (half + 1)],
                            start=(ft == 0), stop=(ft == NT - 1)))
                gu_t = inc(last_mm, "cT")
                for wslot, toks in wtk:
                    wg_used[wslot] = gu_t
                nxt = it_g + 1
                if nxt < N_IT:
                    for w in range(2):
                        step_idx = 2 * nxt + w
                        tok = gu_pre[0]
                        if step_idx + 1 < len(gu_steps):
                            if itl == DN_BLK - 1 and w == 1:
                                pend[0] = step_idx + 1
                            else:
                                gu_pre[0] = gu_load(step_idx + 1)
                        prepped.append(gu_prep(step_idx, tok))
                wait("s", "cT", gu_t)
                wait("s", "cV", hmul_hist[-1])
                si_s = 0
                for half in range(2):
                    tgt = pset[half]
                    isc = sc(lambda h, half=half, tgt=tgt: h.activation(
                        out=gbuf[:, 512 * half:512 * (half + 1)], in_=tgt,
                        func=AF.Silu))
                    si_s = inc(isc, "cS")
                silu_last[0] = si_s
                wait("v", "cT", gu_t)
                wait("v", "cS", si_s)
                drain("v")
                if blk > 0 and itl == 0:
                    wait("v", "cT", prev_dn_t)
                for half in range(2):
                    tgt = pset[2 + half]
                    iv = v(lambda h, itl=itl, half=half, tgt=tgt: h.tensor_tensor(
                        out=big[:, 2 * itl, 1024 + 512 * half:1024 + 512 * (half + 1)],
                        in0=gbuf[:, 512 * half:512 * (half + 1)],
                        in1=tgt, op=ALU.mult))
                    hmul_hist.append(inc(iv, "cV"))

            wait("t", "cV", hmul_hist[-1])
            add_hist = []
            it_i = 0
            deq_cv = []
            for seg in range(8):
                slot = seg % 2
                wait("y", "cV", fconsumed[slot])
                tok = dma("dL", lambda h, seg=seg, blk=blk, slot=slot: h.dma_start(
                    out=fslots[slot].rearrange("p (a c) -> p a c", c=256),
                    in_=dw_h.ap()[1024 * blk:1024 * (blk + 1),
                                  256 * seg:256 * (seg + 1)].rearrange(
                        "(a p) c -> p a c", p=128)))
                waitt("v", tok)
                if seg == 0:
                    wait("v", "cT", prev_dn_t)
                iv = v(lambda h, seg=seg, slot=slot: h.tensor_scalar(
                    out=wdT[:, :, 256 * seg:256 * (seg + 1)],
                    in0=fslots[slot].rearrange("p (a c) -> p a c", c=256),
                    scalar1=128.0, scalar2=None, op0=ALU.subtract))
                cv = inc(iv, "cV")
                fconsumed[slot] = cv
                deq_cv.append(cv)
            mul_cv = 0
            for ec in range(4):
                wait("t", "cV", deq_cv[2 * ec + 1])
                for tt in range(NOWN):
                    tgt = ps_b if it_i % 2 == 0 else ps_c
                    if len(add_hist) >= 2:
                        wait("t", "cV", add_hist[-2])
                    last_mm = NULL
                    for jt in range(DN_BLK):
                        last_mm = pe(lambda h, tt=tt, ec=ec, jt=jt, tgt=tgt: h.matmul(
                            out=tgt[:, :],
                            lhsT=big[:, 2 * jt, 1024 + 128 * tt:1024 + 128 * (tt + 1)],
                            rhs=wdT[:, jt, 512 * ec:512 * (ec + 1)],
                            start=(jt == 0), stop=(jt == DN_BLK - 1)))
                    dn_t = inc(last_mm, "cT")
                    wait("v", "cT", dn_t)
                    iv = v(lambda h, ec=ec, tgt=tgt: h.tensor_tensor(
                        out=tgt[:, :], in0=tgt[:, :],
                        in1=sexp[:, 512 * ec:512 * (ec + 1)], op=ALU.mult))
                    mul_cv = inc(iv, "cV")
                    wait("v", "cV", mul_cv)
                    iv = v(lambda h, tt=tt, ec=ec, tgt=tgt: h.tensor_tensor(
                        out=acc[:, tt, 512 * ec:512 * (ec + 1)],
                        in0=acc[:, tt, 512 * ec:512 * (ec + 1)],
                        in1=tgt[:, :], op=ALU.add))
                    add_hist.append(inc(iv, "cV"))
                    it_i += 1
            last_down_add[0] = add_hist[-1]
            prev_dn_t = cnt["cT"]
            if pend[0] is not None:
                gu_pre[0] = gu_load(pend[0])

        ffn_done = cnt["cV"]
        wait("y", "cV", ffn_done)
        for t in range(NOWN):
            st = sy(lambda h, t=t: h.dma_start(
                out=out_h.ap()[128 * t:128 * (t + 1), :], in_=acc[:, t, :]))
            inc(st, "dO", 16)
        wait("y", "dO", cnt["dO"])

    es_a = ExitStack()

    def mka(name, shape, dt):
        return es_a.enter_context(nc.sbuf_tensor(name, shape, dt))

    U = mka("U", [128, 2 * WMAX], BF16)
    TA = {
        "xnownT": mka("xnownT", [128, NT, 1024], BF16),
        "kT": mka("kT", [128, 2052], BF16),
        "vb": mka("vb", [128, 17, HD], BF16),
        "qTb": mka("qTb", [128, 1024], BF16),
        "aoT": mka("aoT", [128, 2, 1024], BF16),
        "U": U,
        "ext": mka("ext", [128, 2 * KTOP], F32),
        "mask": mka("mask", [128, 260], F32),
        "smask": mka("smask", [128, 2052], F32),
        "rs2": mka("rs2", [128, 1796], F32),
        "rs3": mka("rs3", [128, 1540], F32),
        "kqc": mka("kqc", [128, 1024], BF16),
    }

    snap_a = None
    with nc.Block("phase_a") as block:
        @block.sync
        def _(h):
            state["eng"], state["h"] = "y", h
            for k in cnt:
                cnt[k] = 0
            reset_rr()
            program_a(TA)

        @block.vector
        def _(h):
            state["eng"], state["h"] = "v", h
            for k in cnt:
                cnt[k] = 0
            reset_rr()
            program_a(TA)

        @block.scalar
        def _(h):
            state["eng"], state["h"] = "s", h
            for k in cnt:
                cnt[k] = 0
            reset_rr()
            program_a(TA)

        @block.tensor
        def _(h):
            state["eng"], state["h"] = "t", h
            for k in cnt:
                cnt[k] = 0
            reset_rr()
            program_a(TA)

        @block.gpsimd
        def _(h):
            state["eng"], state["h"] = "g", h
            for k in cnt:
                cnt[k] = 0
            reset_rr()
            program_a(TA)
    snap_a = dict(cnt)
    rr_a = dict(rr)
    barrier = (snap_a["cV"], snap_a["cS"], snap_a["cT"])

    es_a.close()

    es_b = ExitStack()

    def mkb(name, shape, dt):
        return es_b.enter_context(nc.sbuf_tensor(name, shape, dt))

    TB = {
        "fraw": mkb("fraw", [128, E], I32),
        "fbf": mkb("fbf", [128, E], BF16),
        "fbf2": mkb("fbf2", [128, E], BF16),
        "wgT": mkb("wgT", [128, 4, NT, 128], BF16),
        "wdT": mkb("wdT", [128, DN_BLK, E], BF16),
        "gbuf": mkb("gbuf", [128, 1024], BF16),
        "sexp": mkb("sexp", [128, E], F32),
        "onesB": mkb("onesB", [1, 128], F32),
    }

    with nc.Block("phase_b") as block:
        @block.sync
        def _(h):
            state["eng"], state["h"] = "y", h
            cnt.update(snap_a)
            rr.update(rr_a)
            program_b(TB, barrier)

        @block.vector
        def _(h):
            state["eng"], state["h"] = "v", h
            cnt.update(snap_a)
            rr.update(rr_a)
            program_b(TB, barrier)

        @block.scalar
        def _(h):
            state["eng"], state["h"] = "s", h
            cnt.update(snap_a)
            rr.update(rr_a)
            program_b(TB, barrier)

        @block.tensor
        def _(h):
            state["eng"], state["h"] = "t", h
            cnt.update(snap_a)
            rr.update(rr_a)
            program_b(TB, barrier)

        @block.gpsimd
        def _(h):
            state["eng"], state["h"] = "g", h
            cnt.update(snap_a)
            rr.update(rr_a)
            program_b(TB, barrier)

    es_b.close()
    es_glob.close()
    return nc


_NC_CACHE = None


def kernel(L0_attn_x, L0_attn_Wq, L0_attn_bq, L0_attn_Wk, L0_attn_bk,
           L0_attn_Wv, L0_attn_bv, L0_attn_Wo, L0_attn_bo, L0_attn_pk,
           L0_attn_pv, L0_ffn_gate_w, L0_ffn_gate_scale, L0_ffn_up_w,
           L0_ffn_up_scale, L0_ffn_down_w, L0_ffn_down_scale):
    global _NC_CACHE
    x = np.ascontiguousarray(np.asarray(L0_attn_x, dtype=np.float32))
    Wq = np.ascontiguousarray(np.asarray(L0_attn_Wq, dtype=np.float32))
    Wk = np.ascontiguousarray(np.asarray(L0_attn_Wk, dtype=np.float32))
    Wv = np.ascontiguousarray(np.asarray(L0_attn_Wv, dtype=np.float32))
    Wo = np.ascontiguousarray(np.asarray(L0_attn_Wo, dtype=np.float32))
    bq = np.asarray(L0_attn_bq, dtype=np.float32)
    bk = np.asarray(L0_attn_bk, dtype=np.float32)
    bv = np.asarray(L0_attn_bv, dtype=np.float32)
    bo = np.asarray(L0_attn_bo, dtype=np.float32)
    pk = np.asarray(L0_attn_pk, dtype=np.float32)[0]
    pv = np.asarray(L0_attn_pv, dtype=np.float32)[0]
    gw = np.ascontiguousarray(np.asarray(L0_ffn_gate_w, dtype=np.int32))
    uw = np.ascontiguousarray(np.asarray(L0_ffn_up_w, dtype=np.int32))
    dwT = np.ascontiguousarray(np.asarray(L0_ffn_down_w, dtype=np.int32).T)
    gs = np.asarray(L0_ffn_gate_scale, dtype=np.float32)
    us = np.asarray(L0_ffn_up_scale, dtype=np.float32)
    ds = np.asarray(L0_ffn_down_scale, dtype=np.float32)

    bqkT = np.ascontiguousarray(
        np.concatenate([bq.reshape(NT, 128).T, bk.reshape(NT, 128).T], axis=1))
    pkT = np.ascontiguousarray(pk.T.reshape(NT, 128, P).transpose(1, 0, 2).reshape(128, NT * P))
    gsT = np.ascontiguousarray(gs.reshape(N_IT, 128).T)
    usT = np.ascontiguousarray(us.reshape(N_IT, 128).T)

    if _NC_CACHE is None:
        _NC_CACHE = build_program()
    nc = _NC_CACHE

    in_maps = []
    for c in range(8):
        b, qh = c // 2, c % 2
        own_tiles = [2 * j + qh for j in range(NOWN)]
        xb = np.ascontiguousarray(x[b])
        xo = np.ascontiguousarray(
            np.concatenate([xb[128 * t:128 * (t + 1)] for t in own_tiles], axis=0))
        qoff = np.full((128, 1), 128.0 * qh, dtype=np.float32)
        in_maps.append({
            "xfull": xb, "xown": xo, "Wq": Wq, "Wk": Wk, "Wv": Wv, "Wo": Wo,
            "bqkT": bqkT, "bv_r": bv.reshape(1, E), "bo_r": bo.reshape(1, E),
            "pkT": pkT, "pv_r": np.ascontiguousarray(pv), "qoff": qoff,
            "gw": gw, "uw": uw, "dwT": dwT, "gsT": gsT, "usT": usT,
            "ds_r": ds.reshape(1, E),
        })

    res = run_bass_kernel_spmd(nc, in_maps, core_ids=list(range(8)))
    out = np.empty((B, S, E), dtype=np.float32)
    for c in range(8):
        b, qh = c // 2, c % 2
        o = np.asarray(res.results[c]["out"], dtype=np.float32)
        for j in range(NOWN):
            t = 2 * j + qh
            out[b, 128 * t:128 * (t + 1), :] = o[128 * j:128 * (j + 1), :]
    return out


if __name__ == "__main__":
    nc = build_program()
    print("built ok")
